# version 10 (final)
# speedup vs baseline: 1.0827x; 1.0037x over previous
import os
import math
import numpy as np
from contextlib import ExitStack

import concourse.bass as bass
import concourse.mybir as mybir
from concourse.bass_utils import run_bass_kernel_spmd

F32 = mybir.dt.float32
BF16 = mybir.dt.bfloat16
I32 = mybir.dt.int32
AX = mybir.AxisListType.X
AL = mybir.AluOpType
AF = mybir.ActivationFunctionType

T = 2048
D = 1024
NT = T // 128
NPT = D // 128
NH = 16
HD = 64
S = 4096
DFF0 = 2816
NIC0 = DFF0 // 128
DFF1 = 4096
RK = 256
QMAX = 127.0
RNE = 12582912.0
EPS = 1e-5
ENT = 0.5
DECAY = 0.1
SUPC = 0.3
SCALE_ATT = 0.125
BSHIFT = 40.0

DEV_STAGES = int(os.environ.get("DEV_STAGES", "6"))


class Sem:
    def __init__(self, h):
        self.h = h
        self.n = 0
        self.wmax = 0

    def tick(self, k=1):
        self.n += k
        return self.n


class EpochSem:

    def __init__(self, subs):
        self.subs = subs
        self.i = 0

    @property
    def h(self):
        return self.subs[self.i].h

    @property
    def n(self):
        return self.subs[self.i].n

    def tick(self, k=1):
        return self.subs[self.i].tick(k)

    def advance(self):
        self.i = (self.i + 1) % len(self.subs)


class B:
    ENGINES = ("sync", "vector", "scalar", "tensor", "gpsimd")

    def __init__(self, nc, ctx):
        self.nc = nc
        self.ctx = ctx
        self.q = {e: [] for e in self.ENGINES}
        self.sems = {}
        self.ct = {}
        self.esems = []
        self.wseen = {e: {} for e in self.ENGINES}
        self.wknow = {e: {} for e in self.ENGINES}

    def sem(self, name):
        if name not in self.sems:
            self.sems[name] = Sem(self.ctx.enter_context(self.nc.semaphore(name)))
        return self.sems[name]

    @staticmethod
    def _rec(s, v):
        t = s.subs[s.i] if isinstance(s, EpochSem) else s
        t.wmax = max(t.wmax, v)

    def op(self, eng, fn, waits=(), inc=None, inc_n=1):
        c = self.ct.get(eng)
        seen = self.wseen[eng]
        know = self.wknow[eng]
        wl = []
        if c is not None:
            wl.append((c.h, c.n))
        for (s, v) in waits:
            self._rec(s, v)
            k = id(s.h)
            if seen.get(k, -1) < v:
                seen[k] = v
                know[k] = max(know.get(k, -1), v)
                wl.append((s.h, v))
        if inc is None and c is not None:
            inc = c
        tok_sem = inc
        tok = inc.tick(inc_n) if inc is not None else None
        hh = inc.h if inc is not None else None

        def g(e, fn=fn, wl=wl, hh=hh, inc_n=inc_n):
            for h, v in wl:
                e.wait_ge(h, v)
            ins = fn(e)
            if hh is not None and ins is not None:
                ins.then_inc(hh, inc_n)
        self.q[eng].append(g)
        return (tok_sem, tok)

    def dma(self, eng, fn, waits=(), inc=None):
        know = self.wknow[eng]
        wl = []
        for (s, v) in waits:
            self._rec(s, v)
            know[id(s.h)] = max(know.get(id(s.h), -1), v)
            wl.append((s.h, v))
        t = inc.subs[inc.i] if isinstance(inc, EpochSem) else inc
        if t.wmax > 0 and know.get(id(t.h), -1) < t.wmax:
            know[id(t.h)] = t.wmax
            wl.append((t.h, t.wmax))
        tok = inc.tick(16)
        hh = inc.h

        def g(e, fn=fn, wl=wl, hh=hh):
            for h, v in wl:
                e.wait_ge(h, v)
            fn(e).then_inc(hh, 16)
        self.q[eng].append(g)
        return (inc, tok)

    def waits(self, eng, waits):
        know = self.wknow[eng]
        wl = []
        for (s, v) in waits:
            self._rec(s, v)
            know[id(s.h)] = max(know.get(id(s.h), -1), v)
            wl.append((s.h, v))

        def g(e, wl=wl):
            for h, v in wl:
                e.wait_ge(h, v)
        self.q[eng].append(g)

    def esem(self, name, k=4):
        es = EpochSem([self.sem(f"{name}_{i}") for i in range(k)])
        self.esems.append(es)
        return es

    def barrier(self):
        snap = [(s, s.n) for s in self.sems.values() if s.n > 0]
        for e in self.ENGINES:
            self.waits(e, snap)
        for es in self.esems:
            es.advance()


def build_nc(n_stages):
    nc = bass.Bass("TRN2", target_bir_lowering=False, debug=False)

    def inp(name, shape, dtype=F32):
        return nc.dram_tensor(name, list(shape), dtype, kind="ExternalInput").ap()

    x_in = inp("x_in", (T, D))
    wg = inp("wg", (DFF0, D))
    wu = inp("wu", (DFF0, D))
    wdn = inp("wdn", (D, DFF0))
    wgT = inp("wgT", (D, DFF0))
    wuT = inp("wuT", (D, DFF0))
    wdnT = inp("wdnT", (DFF0, D))
    clips = inp("clips", (1, 2))
    eye_c = inp("eye_c", (128, 128))
    y_out = nc.dram_tensor("y_out", [T, D], F32, kind="ExternalOutput").ap()

    wqkv_blk = inp("wqkv_blk", (3 * NPT, 128, 128))
    w1_l0 = inp("w1_l0", (D, D))
    w2_l0 = inp("w2_l0", (D, D))
    gnrows = inp("gnrows", (2, D))
    gmaskT = inp("gmaskT", (NH, 128, 128))
    gkvec = inp("gkvec", (128, NH))
    gqvec = inp("gqvec", (128, NH))
    gcpow = inp("gcpow", (1, NH))
    corrg = inp("corrg", (128, NT * NH))

    U1_i = inp("U1_i", (D, RK))
    V1_i = inp("V1_i", (RK, DFF1))
    U2_i = inp("U2_i", (DFF1, RK))
    V2_i = inp("V2_i", (RK, D))
    b1r = inp("b1r", (1, DFF1))
    b2r = inp("b2r", (1, D))

    WqT = inp("WqT", (D, D)); WkT = inp("WkT", (D, D)); WvT = inp("WvT", (D, D))
    WoT = inp("WoT", (D, D)); WeqT = inp("WeqT", (D, D)); WekT = inp("WekT", (D, D))
    battn = inp("battn", (6, D))
    apw = inp("apw", (D, 32))
    apb = inp("apb", (1, 32))
    wv_rows = inp("wv_rows", (D, D))
    mixwT = inp("mixwT", (NH, 64, 64))
    mixwrT = inp("mixwrT", (NH, 64, 64))
    mixbr = inp("mixbr", (1, D))
    lnrows = inp("lnrows", (2, D))
    fq_dn = inp("fq_dn", (128, 512))
    fq_up = inp("fq_up", (128, 512))
    fek_up = inp("fek_up", (128, 512))
    fek_dn = inp("fek_dn", (128, 512))
    tdiag_c = inp("tdiag_c", (128, 128))
    koffs = inp("koffs", (1, 128), I32)
    blkones = inp("blkones", (NPT, 128, NH))

    wg_q = nc.dram_tensor("wg_q", [NPT * 128, DFF0], BF16, kind="Internal")
    wu_q = nc.dram_tensor("wu_q", [NPT * 128, DFF0], BF16, kind="Internal")
    wd_q = nc.dram_tensor("wd_q", [NIC0 * 128, D], BF16, kind="Internal")
    scl_d = nc.dram_tensor("scl_d", [3, DFF0], F32, kind="Internal")
    h_sp = nc.dram_tensor("h_sp", [NIC0 * 128, T], BF16, kind="Internal")
    u_sp = nc.dram_tensor("u_sp", [NT * 128, D], BF16, kind="Internal")
    qhat_d = nc.dram_tensor("qhat_d", [4 * NH * 65, 512], BF16, kind="Internal")
    eqT_d = nc.dram_tensor("eqT_d", [4 * NPT * 128, 512], BF16, kind="Internal")
    mx_in = nc.dram_tensor("mx_in", [1, 1], F32, kind="Internal")
    mx_out = nc.dram_tensor("mx_out", [1, 1], F32, kind="Internal")
    st_in = nc.dram_tensor("st_in", [1, NH * 64 * 64], F32, kind="Internal")
    st_out = nc.dram_tensor("st_out", [2, NH * 64 * 64], F32, kind="Internal")
    EX_K = NH * 65 * T
    EX_E = NH * 64 * T
    EX_M = 8 * NT * 128 * 130
    EX_TOT = EX_K + 2 * EX_E + EX_M
    ex_in = nc.dram_tensor("ex_in", [1, EX_TOT], BF16, kind="Internal")
    ex_out = nc.dram_tensor("ex_out", [2, EX_TOT], BF16, kind="Internal")
    sx_in = nc.dram_tensor("sx_in", [1, 32], F32, kind="Internal")
    v1b_d = nc.dram_tensor("v1b_d", [256, DFF1], BF16, kind="Internal")
    nrm_d = nc.dram_tensor("nrm_d", [4 * NH, T], F32, kind="Internal")
    wsup_d = nc.dram_tensor("wsup_d", [D, D], BF16, kind="Internal")
    den_d = nc.dram_tensor("den_d", [NH, T], F32, kind="Internal")
    exr_d = nc.dram_tensor("exr_d", [EX_TOT], BF16, kind="Internal")
    escm_in = inp("escm", (128, 34))
    u2b_d = nc.dram_tensor("u2b_d", [DFF1, 256], BF16, kind="Internal")
    sx_out = nc.dram_tensor("sx_out", [2, 32], F32, kind="Internal")

    with ExitStack() as ctx:
        b = B(nc, ctx)
        sb = lambda n, s, d: ctx.enter_context(nc.sbuf_tensor(n, list(s), d))
        psb = lambda n, s, d=F32: ctx.enter_context(nc.psum_tensor(n, list(s), d))

        xres = sb("xres", (128, NT * D), F32)
        ub = sb("ub", (128, NT * D), BF16)
        uT = sb("uT", (128, NPT * T), BF16)
        eye = sb("eye", (128, 128), BF16)
        stats = sb("stats", (128, 128), F32)
        srow = sb("srow", (8, 512), F32)
        arena = sb("arena", (128, 2 * DFF0), F32)
        wstgF = arena[:, 0:DFF0]
        wrcp = arena[:, DFF0:2 * DFF0]
        wdcache = arena[:, 0:2 * DFF0].bitcast(BF16)
        wdeq = sb("wdeq", (128, D), F32)
        wstgB = sb("wstgB", (128, 3072), BF16)
        sdeq = sb("sdeq", (128, 48), F32)
        wbufA = sb("wbufA", (128, 2048), BF16)
        wbufB = sb("wbufB", (128, 2048), BF16)
        stg = sb("stg", (128, 1024), F32)
        stg2 = sb("stg2", (128, 1024), F32)
        stgB = sb("stgB", (128, 2048), BF16)
        bcst = sb("bcst", (128, 2 * D), F32)
        gmask_sb = sb("gmask_sb", (128, NH * 128), BF16)
        gkq_sb = sb("gkq_sb", (128, 2 * NH), F32)
        corr_sb = sb("corr_sb", (128, NT * NH), F32)
        qTc = sb("qTc", (128, NPT * 128), BF16)
        kTc = sb("kTc", (128, NPT * 128), BF16)
        opre = sb("opre", (128, D), BF16)
        pcb = sb("pcb", (128, 128), BF16)
        st_sb = sb("st_sb", (128, 2 * 512), F32)
        escm_sb = sb("escm_sb", (128, 34), F32)
        blkon_sb = sb("blkon_sb", (128, 128), BF16)
        mix_sb = qTc[:, 0:520]
        fek_sb = bcst[:, 1024:2048]
        c12_sb = bcst[:, 0:512]
        fq_sb = bcst[:, 0:1024]
        tdg_sb = bcst[:, 1024:1152]
        kc = stgB[:, 1024:2048]
        st_bA = bcst[:, 0:512].bitcast(BF16)
        st_pB = st_sb[:, 512:1024].bitcast(BF16)
        vc = opre
        ktl = stgB[:, 0:1024]
        retf = wdeq
        retb = qTc
        opreT = kTc
        psum = [psb(f"ps{i}", (128, 512)) for i in range(7)]
        pstr = psb("pstr", (128, 1024), BF16)

        for e in ("tensor", "vector", "scalar", "gpsimd"):
            b.ct[e] = b.sem("c_" + e)
        ct = b.ct
        d_in = b.esem("d_in", 4)
        d_gp = b.esem("d_gp", 4)
        d_out = b.sem("d_out")
        d_wA = b.sem("d_wA")
        d_wB = b.sem("d_wB")
        wfree = [b.sem("wfree0"), b.sem("wfree1")]
        ccs = b.sem("ccs")
        TEN, VEC, SCA, GPS, SYN = "tensor", "vector", "scalar", "gpsimd", "sync"

        for t in range(NT):
            b.dma(SYN, lambda e, t=t: e.dma_start(
                out=xres[:, t * D:(t + 1) * D], in_=x_in[t * 128:(t + 1) * 128, :]),
                inc=d_in)
        b.dma(GPS, lambda e: e.dma_start(out=eye[:], in_=eye_c), inc=d_gp)
        b.barrier()

        def emit_ln():
            for t in range(NT):
                xt = xres[:, t * D:(t + 1) * D]
                b.op(VEC, lambda e, xt=xt: e.bn_stats(stg[:, 0:6], xt[:, 0:512]))
                b.op(VEC, lambda e, xt=xt: e.bn_stats(stg[:, 6:12], xt[:, 512:1024]))
                vtok = b.op(VEC, lambda e, t=t: e.bn_aggr(
                    stats[:, 2 * t:2 * t + 2], stg[:, 0:12]))
                vtok = b.op(VEC, lambda e, t=t: e.tensor_scalar(
                    stg2[:, 1:2], stats[:, 2 * t + 1:2 * t + 2], EPS, None, AL.add))
                stok = b.op(SCA, lambda e, t=t: e.activation(
                    stg2[:, 0:1], stg2[:, 1:2], AF.Sqrt,
                    bias=0.0, scale=1.0), waits=[vtok])
                b.op(VEC, lambda e, t=t: e.reciprocal(
                    stats[:, 32 + t:33 + t], stg2[:, 0:1]), waits=[stok])
                vtok = b.op(VEC, lambda e, t=t: e.tensor_scalar(
                    stats[:, 48 + t:49 + t], stats[:, 2 * t:2 * t + 1],
                    stats[:, 32 + t:33 + t], -1.0, AL.mult, AL.mult))
                b.op(SCA, lambda e, t=t, xt=xt: e.activation(
                    ub[:, t * D:(t + 1) * D], xt, AF.Identity,
                    bias=stats[:, 48 + t:49 + t], scale=stats[:, 32 + t:33 + t]),
                    waits=[vtok])

        def emit_transpose(src, dst, n_t=NT, src_stride=D, dst_stride=T):
            for t in range(n_t):
                for j in range(NPT):
                    reg = (t * NPT + j) % 8
                    pr = pstr[:, reg * 128:(reg + 1) * 128]
                    ttok = b.op(TEN, lambda e, t=t, j=j, pr=pr, src=src: e.transpose(
                        pr, src[:, t * src_stride + j * 128:t * src_stride + (j + 1) * 128],
                        eye[:]),
                        waits=[(ct[VEC], ct[VEC].n), (ct[SCA], ct[SCA].n)])
                    dsl = dst[:, j * dst_stride + t * 128:j * dst_stride + (t + 1) * 128]
                    if (t + j) % 2 == 0:
                        b.op(VEC, lambda e, dsl=dsl, pr=pr: e.tensor_copy(dsl, pr),
                             waits=[ttok])
                    else:
                        b.op(SCA, lambda e, dsl=dsl, pr=pr: e.copy(dsl, pr),
                             waits=[ttok])

        def emit_global_max(src_col, dst_col):
            gtok = b.dma(GPS, lambda e, src_col=src_col: e.dma_start(
                out=srow[0:1, 0:128], in_=stats[:, src_col:src_col + 1]),
                waits=[(ct[VEC], ct[VEC].n)], inc=d_gp)
            vtok = b.op(VEC, lambda e: e.reduce_max(
                srow[0:1, 128:129], srow[0:1, 0:128], axis=AX), waits=[gtok])
            gtok = b.dma(GPS, lambda e: e.dma_start(
                out=mx_in[:], in_=srow[0:1, 128:129]),
                waits=[vtok], inc=d_gp)
            cctok = b.op(GPS, lambda e: e.collective_compute(
                "AllReduce", AL.max, replica_groups=[list(range(8))],
                ins=[mx_in[:]], outs=[mx_out[:]]),
                waits=[gtok], inc=ccs)
            gtok = b.dma(GPS, lambda e, dst_col=dst_col: e.dma_start(
                out=stats[:, dst_col:dst_col + 1],
                in_=bass.AP(mx_out, 0, [[0, 128], [1, 1]])),
                waits=[cctok], inc=d_gp)
            return gtok

        def emit_wquant():
            gt = b.dma(GPS, lambda e: e.dma_start(out=srow[0:1, 256:258], in_=clips),
                       inc=d_gp)
            st = b.op(SCA, lambda e: e.activation(
                srow[0:1, 258:260], srow[0:1, 256:258], AF.Sigmoid, bias=0.0, scale=1.0),
                waits=[gt])
            gt = b.dma(SYN, lambda e: e.dma_start(
                out=bass.AP(scl_d, 2 * DFF0, [[1, 1], [1, 2]]), in_=srow[0:1, 258:260]),
                waits=[st], inc=d_in)
            b.dma(SYN, lambda e: e.dma_start(
                out=stats[:, 126:128],
                in_=bass.AP(scl_d, 2 * DFF0, [[0, 128], [1, 2]])),
                waits=[gt], inc=d_in)
            b.barrier()
            for wi, (w_oi, w_T, wqd) in enumerate(
                    ((wg, wgT, wg_q), (wu, wuT, wu_q), (wdn, wdnT, wd_q))):
                deq = (wi == 2)
                OUTD = D if deq else DFF0
                IND = DFF0 if deq else D
                n_oc = OUTD // 128
                n_ic = IND // 128
                for oc in range(n_oc):
                    dtok = b.dma(SYN, lambda e, oc=oc, w_oi=w_oi, IND=IND: e.dma_start(
                        out=wstgF[:, 0:IND], in_=w_oi[oc * 128:(oc + 1) * 128, :]),
                        waits=[(ct[VEC], ct[VEC].n)], inc=d_in)
                    b.op(VEC, lambda e, oc=oc, IND=IND: e.reduce_max(
                        stg[:, 16 + oc:17 + oc], wstgF[:, 0:IND], axis=AX,
                        apply_absolute_value=True), waits=[dtok])
                vt = b.op(VEC, lambda e, n_oc=n_oc: e.reciprocal(
                    stg[:, 48:48 + n_oc], stg[:, 16:16 + n_oc]))
                vt = b.op(VEC, lambda e, n_oc=n_oc: e.tensor_scalar(
                    stg[:, 48:48 + n_oc], stg[:, 48:48 + n_oc], QMAX, None, AL.mult))
                if not deq:
                    base = 0 if wi == 0 else 22
                    vt2 = b.op(VEC, lambda e, n_oc=n_oc, base=base: e.tensor_scalar(
                        sdeq[:, base:base + n_oc], stg[:, 16:16 + n_oc],
                        1.0 / QMAX, None, AL.mult))
                else:
                    vt2 = b.op(VEC, lambda e, n_oc=n_oc: e.tensor_scalar(
                        stg[:, 56:56 + n_oc], stg[:, 16:16 + n_oc],
                        1.0 / QMAX, None, AL.mult))
                dtok = b.dma(SYN, lambda e, n_oc=n_oc: e.dma_start(
                    out=bass.AP(scl_d, 0, [[1, 128], [128, n_oc]]),
                    in_=stg[:, 48:48 + n_oc],
                    allow_slow_non_contiguous=True), waits=[vt], inc=d_in)
                dtok = b.dma(SYN, lambda e, OUTD=OUTD: e.dma_start(
                    out=wrcp[:, 0:OUTD],
                    in_=bass.AP(scl_d, 0, [[0, 128], [1, OUTD]])),
                    waits=[dtok], inc=d_in)
                if deq:
                    dtok3 = b.dma(SYN, lambda e, n_oc=n_oc: e.dma_start(
                        out=bass.AP(scl_d, DFF0, [[1, 128], [128, n_oc]]),
                        in_=stg[:, 56:56 + n_oc],
                        allow_slow_non_contiguous=True), waits=[vt2], inc=d_in)
                    b.dma(SYN, lambda e, OUTD=OUTD: e.dma_start(
                        out=wdeq[:, 0:OUTD],
                        in_=bass.AP(scl_d, DFF0, [[0, 128], [1, OUTD]])),
                        waits=[dtok3], inc=d_in)
                b.barrier()
                for icc in range(n_ic):
                    dtok = b.dma(SYN, lambda e, icc=icc, w_T=w_T, OUTD=OUTD: e.dma_start(
                        out=wstgF[:, 0:OUTD], in_=w_T[icc * 128:(icc + 1) * 128, :]),
                        waits=[(ct[VEC], ct[VEC].n), (ct[SCA], ct[SCA].n)], inc=d_in)
                    b.op(VEC, lambda e, OUTD=OUTD: e.tensor_mul(
                        wstgF[:, 0:OUTD], wstgF[:, 0:OUTD], wrcp[:, 0:OUTD]),
                        waits=[dtok])
                    b.op(VEC, lambda e, OUTD=OUTD: e.tensor_scalar(
                        wstgF[:, 0:OUTD], wstgF[:, 0:OUTD], RNE, RNE,
                        AL.add, AL.subtract))
                    if deq:
                        vt = b.op(VEC, lambda e, OUTD=OUTD: e.tensor_mul(
                            wstgB[:, 0:OUTD], wstgF[:, 0:OUTD], wdeq[:, 0:OUTD]))
                    else:
                        vt = b.op(SCA, lambda e, OUTD=OUTD: e.copy(
                            wstgB[:, 0:OUTD], wstgF[:, 0:OUTD]),
                            waits=[(ct[VEC], ct[VEC].n)])
                    b.dma(SYN, lambda e, icc=icc, wqd=wqd, OUTD=OUTD: e.dma_start(
                        out=bass.AP(wqd, icc * 128 * OUTD, [[OUTD, 128], [1, OUTD]]),
                        in_=wstgB[:, 0:OUTD]), waits=[vt], inc=d_in)
                b.barrier()

        def emit_l0ffn(callno):
            emit_ln()
            b.barrier()
            for t in range(NT):
                b.op(VEC, lambda e, t=t: e.reduce_max(
                    stg[:, 64 + t:65 + t], ub[:, t * D:(t + 1) * D], axis=AX,
                    apply_absolute_value=True))
            b.op(VEC, lambda e: e.reduce_max(stats[:, 64:65], stg[:, 64:64 + NT],
                                             axis=AX))
            emit_global_max(64, 65)
            b.barrier()
            b.op(VEC, lambda e: e.tensor_scalar(
                stats[:, 66:67], stats[:, 65:66], stats[:, 126:127], 1.0 / QMAX,
                AL.mult, AL.mult))
            b.op(VEC, lambda e: e.reciprocal(stats[:, 67:68], stats[:, 66:67]))
            b.barrier()
            for t in range(NT):
                usl = ub[:, t * D:(t + 1) * D]
                st = b.op(SCA, lambda e, usl=usl: e.activation(
                    stg[:, 0:D], usl, AF.Identity, bias=0.0, scale=stats[:, 67:68]),
                    waits=[(ct[VEC], ct[VEC].n)])
                b.op(VEC, lambda e: e.tensor_scalar(
                    stg[:, 0:D], stg[:, 0:D], RNE, RNE, AL.add, AL.subtract),
                    waits=[st])
                b.op(VEC, lambda e, usl=usl: e.tensor_scalar(
                    usl, stg[:, 0:D], -(QMAX + 1.0), QMAX, AL.max, AL.min))
            b.barrier()
            emit_transpose(ub, uT)
            b.barrier()
            wslots = [(wbufA, d_wA), (wbufB, d_wB)]
            last_pg_rd = {0: None, 1: None}
            last_pu_rd = {0: None, 1: None}
            spill_prev = None
            for ic in range(NIC0):
                slot = ic % 2
                wb_, dw = wslots[slot]
                wf = wfree[slot]
                b.dma(SYN, lambda e, ic=ic, wb_=wb_: e.dma_start(
                    out=wb_[:, 0:1024],
                    in_=bass.AP(wg_q, ic * 128,
                                [[DFF0, 128], [128 * DFF0, NPT], [1, 128]])),
                    waits=[(wf, wf.n)], inc=dw)
                b.dma(SYN, lambda e, ic=ic, wb_=wb_: e.dma_start(
                    out=wb_[:, 1024:2048],
                    in_=bass.AP(wu_q, ic * 128,
                                [[DFF0, 128], [128 * DFF0, NPT], [1, 128]])),
                    waits=[(wf, wf.n)], inc=dw)
                for tg in range(4):
                    pg = psum[tg % 2]
                    pu = psum[2 + tg % 2]
                    for j in range(NPT):
                        b.op(TEN, lambda e, j=j, tg=tg, pg=pg, wb_=wb_: e.matmul(
                            pg[:], wb_[:, j * 128:(j + 1) * 128],
                            uT[:, j * T + tg * 512:j * T + (tg + 1) * 512],
                            start=(j == 0), stop=(j == NPT - 1)),
                            waits=[(dw, dw.n)] + [t for t in
                                   (last_pg_rd[tg % 2], last_pu_rd[tg % 2])
                                   if t is not None] if j == 0 else ())
                        b.op(TEN, lambda e, j=j, tg=tg, pu=pu, wb_=wb_: e.matmul(
                            pu[:], wb_[:, 1024 + j * 128:1024 + (j + 1) * 128],
                            uT[:, j * T + tg * 512:j * T + (tg + 1) * 512],
                            start=(j == 0), stop=(j == NPT - 1)))
                    mmtok = (ct[TEN], ct[TEN].n)
                    st = b.op(SCA, lambda e, pg=pg: e.activation(
                        stg[:, 0:512], pg[:], AF.Identity, bias=0.0,
                        scale=stats[:, 66:67]),
                        waits=[mmtok, (ct[VEC], ct[VEC].n)])
                    last_pg_rd[tg % 2] = st
                    st = b.op(SCA, lambda e, ic=ic: e.activation(
                        stg[:, 0:512], stg[:, 0:512], AF.Silu, bias=0.0,
                        scale=sdeq[:, ic:ic + 1]))
                    vt = b.op(VEC, lambda e, ic=ic, pu=pu: e.tensor_scalar(
                        stg[:, 512:1024], pu[:], stats[:, 66:67],
                        sdeq[:, 22 + ic:23 + ic], AL.mult, AL.mult), waits=[mmtok])
                    last_pu_rd[tg % 2] = vt
                    vt = b.op(VEC, lambda e, tg=tg: e.tensor_mul(
                        stgB[:, tg * 512:(tg + 1) * 512], stg[:, 0:512],
                        stg[:, 512:1024]),
                        waits=[st] + ([spill_prev] if spill_prev is not None
                                      else []))
                    b.op(VEC, lambda e, ic=ic, tg=tg: e.reduce_max(
                        stg2[:, ic * 4 + tg:ic * 4 + tg + 1],
                        stgB[:, tg * 512:(tg + 1) * 512], axis=AX,
                        apply_absolute_value=True))
                b.op(TEN, lambda e: e.nop(), inc=wf)
                spill_prev = b.dma(SYN, lambda e, ic=ic: e.dma_start(
                    out=bass.AP(h_sp, ic * 128 * T, [[T, 128], [1, T]]),
                    in_=stgB[:, 0:T]), waits=[(ct[VEC], ct[VEC].n)], inc=d_in)
            b.op(VEC, lambda e: e.reduce_max(stats[:, 68:69],
                                             stg2[:, 0:NIC0 * 4], axis=AX))
            emit_global_max(68, 69)
            b.barrier()
            b.op(VEC, lambda e: e.tensor_scalar(
                stats[:, 70:71], stats[:, 69:70], stats[:, 127:128], 1.0 / QMAX,
                AL.mult, AL.mult))
            b.op(VEC, lambda e: e.reciprocal(stats[:, 71:72], stats[:, 70:71]))
            b.op(VEC, lambda e: e.tensor_scalar(
                stats[:, 72:73], stats[:, 70:71], 0.5, None, AL.mult))
            b.barrier()
            d_h = [b.sem("d_h0"), b.sem("d_h1")]
            hfree = [b.sem("hfree0"), b.sem("hfree1")]
            b.op(VEC, lambda e: e.memset(stats[:, 74:75], RNE))
            for oh in range(2):
                for ic in range(NIC0):
                    b.dma(SYN, lambda e, ic=ic, oh=oh: e.dma_start(
                        out=wdcache[:, ic * 512:(ic + 1) * 512],
                        in_=bass.AP(wd_q, ic * 128 * D + oh * 512,
                                    [[D, 128], [1, 512]])),
                        waits=[(ct[TEN], ct[TEN].n)], inc=d_wA)
                b.barrier()
                last_act = {0: None, 1: None}
                last_mm = {0: None, 1: None}
                last_po_rd = {0: None, 1: None}
                sgtok = {0: None, 1: None}
                for tt in range(NT):
                    po = psum[4 + tt % 2]
                    slot = tt % 2
                    raw = ub[:, slot * 8192:slot * 8192 + DFF0]
                    hq = ub[:, slot * 8192 + 4096:slot * 8192 + 4096 + DFF0]
                    lt = b.dma(SYN, lambda e, tt=tt, raw=raw: e.dma_start(
                        out=raw,
                        in_=bass.AP(h_sp, tt * 128,
                                    [[T, 128], [128 * T, NIC0], [1, 128]])),
                        waits=[last_act[slot]] if last_act[slot] else [],
                        inc=d_h[slot])
                    chunk_tok = []
                    for c in range(6):
                        c0 = c * 512
                        cw = min(512, DFF0 - c0)
                        sg = stg[:, (c % 2) * 512:(c % 2) * 512 + cw]
                        at = b.op(SCA, lambda e, raw=raw, sg=sg, c0=c0, cw=cw:
                                  e.activation(
                            sg, raw[:, c0:c0 + cw], AF.Identity,
                            bias=stats[:, 74:75], scale=stats[:, 71:72]),
                            waits=[lt] + ([sgtok[c % 2]] if sgtok[c % 2]
                                          else []))
                        b.op(VEC, lambda e, sg=sg: e.tensor_scalar(
                            sg, sg, -RNE, QMAX, AL.add, AL.min), waits=[at])
                        vt = b.op(VEC, lambda e, sg=sg, hq=hq, c0=c0, cw=cw:
                                  e.tensor_scalar(
                            hq[:, c0:c0 + cw], sg, -(QMAX + 1.0), None, AL.max),
                            waits=[last_mm[slot]] if (c == 0 and last_mm[slot])
                            else [])
                        sgtok[c % 2] = vt
                        chunk_tok.append(vt)
                        if c == 5:
                            last_act[slot] = at
                    for ic in range(NIC0):
                        mmt = b.op(TEN, lambda e, ic=ic, po=po, hq=hq: e.matmul(
                            po[:], hq[:, ic * 128:(ic + 1) * 128],
                            wdcache[:, ic * 512:(ic + 1) * 512],
                            start=(ic == 0), stop=(ic == NIC0 - 1)),
                            waits=[chunk_tok[ic // 4]]
                            + ([last_po_rd[tt % 2]]
                               if ic == 0 and last_po_rd[tt % 2] else []))
                    last_mm[slot] = mmt
                    mmtok = (ct[TEN], ct[TEN].n)
                    pot = b.op(VEC, lambda e, tt=tt, oh=oh, po=po:
                               e.scalar_tensor_tensor(
                        xres[:, tt * D + oh * 512:tt * D + (oh + 1) * 512],
                        po[:], stats[:, 72:73],
                        xres[:, tt * D + oh * 512:tt * D + (oh + 1) * 512],
                        AL.mult, AL.add), waits=[mmtok])
                    last_po_rd[tt % 2] = pot
                b.barrier()


        def emit_l0attn():
            emit_ln()
            b.barrier()
            emit_transpose(ub, uT)
            b.barrier()
            for i in range(3 * NPT):
                b.dma(GPS, lambda e, i=i: e.dma_start(
                    out=wstgB[:, i * 128:(i + 1) * 128], in_=wqkv_blk[i]), inc=d_gp)
            w1b = wdcache
            for j in range(NPT):
                b.dma(GPS, lambda e, j=j: e.dma_start(
                    out=w1b[:, j * D:(j + 1) * D], in_=w1_l0[j * 128:(j + 1) * 128, :]),
                    inc=d_gp)
            w2b = ub
            for j in range(NPT):
                b.dma(GPS, lambda e, j=j: e.dma_start(
                    out=w2b[:, j * D:(j + 1) * D], in_=w2_l0[j * 128:(j + 1) * 128, :]),
                    inc=d_gp)
            for h in range(NH):
                b.dma(GPS, lambda e, h=h: e.dma_start(
                    out=gmask_sb[:, h * 128:(h + 1) * 128], in_=gmaskT[h]), inc=d_gp)
            b.dma(SYN, lambda e: e.dma_start(out=gkq_sb[:, 0:NH], in_=gkvec), inc=d_in)
            b.dma(SYN, lambda e: e.dma_start(out=gkq_sb[:, NH:2 * NH], in_=gqvec), inc=d_in)
            b.dma(SYN, lambda e: e.dma_start(out=corr_sb[:], in_=corrg), inc=d_in)
            b.op(VEC, lambda e: e.memset(st_sb[:], 0.0))
            b.op(VEC, lambda e: e.memset(st_bA[:, 0:1024], 0.0))
            b.barrier()
            gam_np = _gammas().astype(np.float64)

            def produce_qkv(c, qdst, kTdst, kdst, vdst, qonly=False):
                for j in range(NPT):
                    pq = psum[j % 2]
                    b.op(TEN, lambda e, j=j, c=c, pq=pq, qonly=qonly: e.matmul(
                        pq[:, 0:128], wstgB[:, j * 128:(j + 1) * 128],
                        uT[:, j * T + c * 128:j * T + (c + 1) * 128],
                        start=True, stop=qonly, skip_group_check=True),
                        waits=[(ct[VEC], ct[VEC].n), (ct[SCA], ct[SCA].n)])
                    if qonly:
                        mmt = (ct[TEN], ct[TEN].n)
                        eng = VEC if j % 2 == 0 else SCA
                        if eng == VEC:
                            b.op(VEC, lambda e, j=j, pq=pq, qdst=qdst: e.tensor_copy(
                                qdst[:, j * 128:(j + 1) * 128], pq[:, 0:128]),
                                waits=[mmt])
                        else:
                            b.op(SCA, lambda e, j=j, pq=pq, qdst=qdst: e.copy(
                                qdst[:, j * 128:(j + 1) * 128], pq[:, 0:128]),
                                waits=[mmt])
                        continue
                    b.op(TEN, lambda e, j=j, c=c, pq=pq: e.matmul(
                        pq[:, 128:256], wstgB[:, (NPT + j) * 128:(NPT + j + 1) * 128],
                        uT[:, j * T + c * 128:j * T + (c + 1) * 128],
                        start=False, stop=False, skip_group_check=True))
                    b.op(TEN, lambda e, j=j, c=c, pq=pq: e.matmul(
                        pq[:, 256:384], uT[:, j * T + c * 128:j * T + (c + 1) * 128],
                        wstgB[:, (NPT + j) * 128:(NPT + j + 1) * 128],
                        start=False, stop=False, skip_group_check=True))
                    b.op(TEN, lambda e, j=j, c=c, pq=pq: e.matmul(
                        pq[:, 384:512], uT[:, j * T + c * 128:j * T + (c + 1) * 128],
                        wstgB[:, (2 * NPT + j) * 128:(2 * NPT + j + 1) * 128],
                        start=False, stop=True, skip_group_check=True))
                    mmt = (ct[TEN], ct[TEN].n)
                    eng = VEC if j % 2 == 0 else SCA
                    if eng == VEC:
                        b.op(VEC, lambda e, j=j, pq=pq, qdst=qdst, kTdst=kTdst: e.tensor_copy(
                            qdst[:, j * 128:(j + 1) * 128], pq[:, 0:128]), waits=[mmt])
                        b.op(VEC, lambda e, j=j, pq=pq, kTdst=kTdst: e.tensor_copy(
                            kTdst[:, j * 128:(j + 1) * 128], pq[:, 128:256]))
                        b.op(VEC, lambda e, j=j, pq=pq, kdst=kdst: e.tensor_copy(
                            kdst[:, j * 128:(j + 1) * 128], pq[:, 256:384]))
                        b.op(VEC, lambda e, j=j, pq=pq, vdst=vdst: e.tensor_copy(
                            vdst[:, j * 128:(j + 1) * 128], pq[:, 384:512]))
                    else:
                        b.op(SCA, lambda e, j=j, pq=pq, qdst=qdst: e.copy(
                            qdst[:, j * 128:(j + 1) * 128], pq[:, 0:128]), waits=[mmt])
                        b.op(SCA, lambda e, j=j, pq=pq, kTdst=kTdst: e.copy(
                            kTdst[:, j * 128:(j + 1) * 128], pq[:, 128:256]))
                        b.op(SCA, lambda e, j=j, pq=pq, kdst=kdst: e.copy(
                            kdst[:, j * 128:(j + 1) * 128], pq[:, 256:384]))
                        b.op(SCA, lambda e, j=j, pq=pq, vdst=vdst: e.copy(
                            vdst[:, j * 128:(j + 1) * 128], pq[:, 384:512]))

            for c in range(NT):
                produce_qkv(c, qTc, kTc, kc, vc)
                qkvtok = [(ct[VEC], ct[VEC].n), (ct[SCA], ct[SCA].n)]
                for h in range(NH):
                    b.op(VEC, lambda e, h=h: e.tensor_scalar(
                        ktl[:, h * 64:(h + 1) * 64], kc[:, h * 64:(h + 1) * 64],
                        gkq_sb[:, h:h + 1], None, AL.mult), waits=qkvtok)
                for h in range(NH):
                    hb = (h % 2) * 64
                    hq = h * 64
                    ps_s = psum[2]
                    ps_i = psum[3]
                    ps_c = psum[4]
                    ps_st = psum[5]
                    jj = h // 2
                    b.op(TEN, lambda e, h=h, jj=jj, hb=hb, ps_s=ps_s: e.matmul(
                        ps_s[:, 0:128], kTc[hb:hb + 64, jj * 128:(jj + 1) * 128],
                        qTc[hb:hb + 64, jj * 128:(jj + 1) * 128],
                        start=True, stop=True),
                        waits=[(ct[VEC], ct[VEC].n), (ct[SCA], ct[SCA].n)])
                    pt = b.op(VEC, lambda e, h=h, ps_s=ps_s: e.tensor_mul(
                        pcb[:, 0:128], ps_s[:, 0:128], gmask_sb[:, h * 128:(h + 1) * 128]),
                        waits=[(ct[TEN], ct[TEN].n)])
                    b.op(TEN, lambda e, h=h, hq=hq, ps_i=ps_i: e.matmul(
                        ps_i[:, 0:64], pcb[:, 0:128], vc[:, hq:hq + 64],
                        start=True, stop=True), waits=[pt])
                    b.op(TEN, lambda e, h=h, jj=jj, hb=hb, ps_c=ps_c: e.matmul(
                        ps_c[:, 0:64], qTc[hb:hb + 64, jj * 128:(jj + 1) * 128],
                        st_bA[hb:hb + 64, (h // 2) * 64:(h // 2) * 64 + 64],
                        start=True, stop=True))
                    t1 = b.op(VEC, lambda e, h=h, ps_c=ps_c: e.tensor_scalar(
                        stg2[:, 0:64], ps_c[:, 0:64], gkq_sb[:, NH + h:NH + h + 1],
                        None, AL.mult), waits=[(ct[TEN], ct[TEN].n)])
                    b.op(VEC, lambda e, h=h, hq=hq, ps_i=ps_i: e.tensor_add(
                        retf[:, hq:hq + 64], stg2[:, 0:64], ps_i[:, 0:64]))
                    if hb == 0:
                        b.op(TEN, lambda e, h=h, hq=hq, ps_st=ps_st: e.matmul(
                            ps_st[0:64, 0:64], ktl[:, hq:hq + 64], vc[:, hq:hq + 64],
                            start=True, stop=True), waits=[(ct[VEC], ct[VEC].n)])
                    else:
                        b.op(TEN, lambda e, h=h, hq=hq, ps_st=ps_st: e.matmul(
                            ps_st[64:128, 0:64], ktl[:, hq:hq + 64], vc[:, hq:hq + 64],
                            start=True, stop=True, tile_position=(0, 64)),
                            waits=[(ct[VEC], ct[VEC].n)])
                    gC = float(gam_np[h] ** 128)
                    b.op(VEC, lambda e, h=h, hb=hb, gC=gC, ps_st=ps_st: e.scalar_tensor_tensor(
                        st_sb[hb:hb + 64, (h // 2) * 64:(h // 2) * 64 + 64],
                        st_sb[hb:hb + 64, (h // 2) * 64:(h // 2) * 64 + 64],
                        gC, ps_st[hb:hb + 64, 0:64], AL.mult, AL.add),
                        waits=[(ct[TEN], ct[TEN].n)])
                    b.op(VEC, lambda e, h=h, hb=hb: e.tensor_copy(
                        st_bA[hb:hb + 64, (h // 2) * 64:(h // 2) * 64 + 64],
                        st_sb[hb:hb + 64, (h // 2) * 64:(h // 2) * 64 + 64]))
                vt = b.op(VEC, lambda e: e.tensor_copy(retb[:], retf[:]))
                b.dma(SYN, lambda e, c=c: e.dma_start(
                    out=bass.AP(u_sp, c * 128 * D, [[D, 128], [1, D]]), in_=retb[:]),
                    waits=[vt], inc=d_in)
                b.waits(VEC, [(d_in, d_in.n)])
            b.barrier()
            for h in range(NH):
                hb = (h % 2) * 64
                b.dma(SYN, lambda e, h=h, hb=hb: e.dma_start(
                    out=bass.AP(st_in, h * 64 * 64, [[64, 64], [1, 64]]),
                    in_=st_sb[hb:hb + 64, (h // 2) * 64:(h // 2) * 64 + 64]), inc=d_in)
            b.waits(GPS, [(d_in, d_in.n)])
            cctok = b.op(GPS, lambda e: e.collective_compute(
                "AllGather", AL.bypass,
                replica_groups=[[0, 1], [2, 3], [4, 5], [6, 7]],
                ins=[st_in[:]], outs=[st_out[:]]), inc=ccs)
            for h in range(NH):
                hb = (h % 2) * 64
                b.dma(GPS, lambda e, h=h, hb=hb: e.dma_start(
                    out=st_pB[hb:hb + 64, (h // 2) * 64:(h // 2) * 64 + 64],
                    in_=bass.AP(st_out, h * 64 * 64, [[64, 64], [1, 64]])),
                    waits=[cctok], inc=d_gp)
            b.barrier()
            b.dma(SYN, lambda e: e.dma_start(
                out=bcst[:, 0:D],
                in_=bass.AP(gnrows.tensor, 0, [[0, 128], [1, D]])), inc=d_in)
            b.dma(SYN, lambda e: e.dma_start(
                out=stg2[:, 8:16],
                in_=bass.AP(gnrows.tensor, D, [[1, 128], [128, NPT]]),
                allow_slow_non_contiguous=True), inc=d_in)
            b.barrier()
            b.op(VEC, lambda e: e.tensor_copy(stgB[:, 1024:1032], stg2[:, 8:16]))
            for oc in range(2):
                pgb = psum[6]
                for j in range(NPT):
                    b.op(TEN, lambda e, j=j, oc=oc, pgb=pgb: e.matmul(
                        pgb[0:1, 0:512], stgB[:, 1024 + j:1025 + j],
                        w2b[:, j * D + oc * 512:j * D + (oc + 1) * 512],
                        start=(j == 0), stop=(j == NPT - 1)),
                        waits=[(ct[VEC], ct[VEC].n)] if j == 0 else ())
                vt = b.op(VEC, lambda e, oc=oc, pgb=pgb: e.tensor_copy(
                    stg2[0:1, oc * 512:(oc + 1) * 512], pgb[0:1, 0:512]),
                    waits=[(ct[TEN], ct[TEN].n)])
            dt = b.dma(SYN, lambda e: e.dma_start(
                out=bass.AP(scl_d, 2 * DFF0 + 16, [[1, 1], [1, D]]),
                in_=stg2[0:1, 0:1024]), waits=[(ct[VEC], ct[VEC].n)], inc=d_in)
            b.dma(SYN, lambda e: e.dma_start(
                out=bcst[:, D:2 * D],
                in_=bass.AP(scl_d, 2 * DFF0 + 16, [[0, 128], [1, D]])),
                waits=[dt], inc=d_in)
            b.barrier()
            for c in range(NT):
                produce_qkv(c, qTc, kTc, kc, vc, qonly=True)
                rlt = b.dma(SYN, lambda e, c=c: e.dma_start(
                    out=kTc[:], in_=bass.AP(u_sp, c * 128 * D, [[D, 128], [1, D]])),
                    waits=[(ct[VEC], ct[VEC].n), (ct[SCA], ct[SCA].n)], inc=d_in)
                stt_tok = {0: None, 1: None}
                for h in range(NH):
                    hb = (h % 2) * 64
                    hq = h * 64
                    jj = h // 2
                    ps_c = psum[4 + h % 2]
                    b.op(TEN, lambda e, h=h, jj=jj, hb=hb, ps_c=ps_c: e.matmul(
                        ps_c[:, 0:64], qTc[hb:hb + 64, jj * 128:(jj + 1) * 128],
                        st_pB[hb:hb + 64, jj * 64:jj * 64 + 64],
                        start=True, stop=True),
                        waits=[(ct[VEC], ct[VEC].n), (ct[SCA], ct[SCA].n)]
                        if h == 0 else
                        [t for t in (stt_tok[h % 2],) if t is not None])
                    stt_tok[h % 2] = b.op(VEC, lambda e, c=c, h=h, hq=hq,
                                          ps_c=ps_c: e.scalar_tensor_tensor(
                        retf[:, hq:hq + 64], ps_c[:, 0:64],
                        corr_sb[:, c * NH + h:c * NH + h + 1],
                        kTc[:, hq:hq + 64], AL.mult, AL.add),
                        waits=[rlt, (ct[TEN], ct[TEN].n)])
                    b.op(VEC, lambda e, hq=hq: e.bn_stats(stg[:, 0:6], retf[:, hq:hq + 64]))
                    b.op(VEC, lambda e: e.bn_aggr(stg[:, 12:14], stg[:, 0:6]))
                    b.op(VEC, lambda e: e.tensor_scalar(
                        stg[:, 15:16], stg[:, 13:14], EPS, None, AL.add))
                    stok = b.op(SCA, lambda e: e.activation(
                        stg[:, 16:17], stg[:, 15:16], AF.Sqrt, bias=0.0, scale=1.0),
                        waits=[(ct[VEC], ct[VEC].n)])
                    b.op(VEC, lambda e: e.reciprocal(stg[:, 17:18], stg[:, 16:17]),
                         waits=[stok])
                    b.op(VEC, lambda e: e.tensor_scalar(
                        stg[:, 18:19], stg[:, 12:13], stg[:, 17:18], -1.0,
                        AL.mult, AL.mult))
                    st2 = b.op(SCA, lambda e, hq=hq: e.activation(
                        stg2[:, 0:64], retf[:, hq:hq + 64], AF.Identity,
                        bias=stg[:, 18:19], scale=stg[:, 17:18]),
                        waits=[(ct[VEC], ct[VEC].n)])
                    b.op(VEC, lambda e, hq=hq: e.tensor_mul(
                        retf[:, hq:hq + 64], stg2[:, 0:64], bcst[:, hq:hq + 64]),
                        waits=[st2])
                for oc in range(2):
                    pg = psum[oc]
                    for j in range(NPT):
                        b.op(TEN, lambda e, j=j, c=c, oc=oc, pg=pg: e.matmul(
                            pg[:], uT[:, j * T + c * 128:j * T + (c + 1) * 128],
                            w1b[:, j * D + oc * 512:j * D + (oc + 1) * 512],
                            start=(j == 0), stop=(j == NPT - 1)),
                            waits=[(ct[VEC], ct[VEC].n)] if j == 0 else ())
                    st3 = b.op(SCA, lambda e, pg=pg: e.activation(
                        stg2[:, 0:512], pg[:], AF.Silu, bias=0.0, scale=1.0),
                        waits=[(ct[TEN], ct[TEN].n)])
                    b.op(VEC, lambda e, oc=oc, pg=pg: e.tensor_add(
                        opre[:, oc * 512:(oc + 1) * 512], stg2[:, 0:512],
                        retf[:, oc * 512:(oc + 1) * 512]), waits=[st3])
                for j in range(NPT):
                    reg = j % 8
                    pr = pstr[:, reg * 128:(reg + 1) * 128]
                    ttok = b.op(TEN, lambda e, j=j, pr=pr: e.transpose(
                        pr, opre[:, j * 128:(j + 1) * 128], eye[:]),
                        waits=[(ct[VEC], ct[VEC].n), (ct[SCA], ct[SCA].n)])
                    if j % 2 == 0:
                        b.op(VEC, lambda e, j=j, pr=pr: e.tensor_copy(
                            opreT[:, j * 128:(j + 1) * 128], pr), waits=[ttok])
                    else:
                        b.op(SCA, lambda e, j=j, pr=pr: e.copy(
                            opreT[:, j * 128:(j + 1) * 128], pr), waits=[ttok])
                for oc in range(2):
                    pg = psum[2 + oc]
                    for j in range(NPT):
                        b.op(TEN, lambda e, j=j, oc=oc, pg=pg: e.matmul(
                            pg[:], opreT[:, j * 128:(j + 1) * 128],
                            w2b[:, j * D + oc * 512:j * D + (oc + 1) * 512],
                            start=(j == 0), stop=(j == NPT - 1)),
                            waits=[(ct[VEC], ct[VEC].n), (ct[SCA], ct[SCA].n)]
                            if j == 0 else ())
                    vt = b.op(VEC, lambda e, oc=oc, pg=pg: e.tensor_add(
                        stg2[:, 0:512], pg[:], bcst[:, D + oc * 512:D + (oc + 1) * 512]),
                        waits=[(ct[TEN], ct[TEN].n)])
                    b.op(VEC, lambda e, c=c, oc=oc: e.tensor_add(
                        xres[:, c * D + oc * 512:c * D + (oc + 1) * 512],
                        xres[:, c * D + oc * 512:c * D + (oc + 1) * 512],
                        stg2[:, 0:512]))
            b.barrier()

        def emit_l1ffn(callno):
            emit_ln()
            b.barrier()
            emit_transpose(ub, uT)
            b.barrier()
            for j in range(NPT):
                b.dma(GPS, lambda e, j=j: e.dma_start(
                    out=wstgB[:, j * 256:(j + 1) * 256],
                    in_=U1_i[j * 128:(j + 1) * 128, :]), inc=d_gp)
            for rc in range(2):
                b.dma(GPS, lambda e, rc=rc: e.dma_start(
                    out=stgB[:, rc * 1024:(rc + 1) * 1024],
                    in_=V2_i[rc * 128:(rc + 1) * 128, :]), inc=d_gp)
            b.dma(SYN, lambda e: e.dma_start(
                out=stg2[:, 32:64],
                in_=bass.AP(b1r.tensor, 0, [[1, 128], [128, 32]]),
                allow_slow_non_contiguous=True), inc=d_in)
            b.dma(GPS, lambda e: e.dma_start(
                out=wstgB[0:1, 2048:3072], in_=b2r), inc=d_gp)
            b.op(VEC, lambda e: e.memset(qTc[0:1, 0:128], 1.0))
            b.barrier()
            PTb = qTc[:, 0:1024]
            b.op(VEC, lambda e: e.memset(pcb[0:1, 0:128], 1.0))
            if callno == 0:
                for rc in range(2):
                    gt = b.dma(GPS, lambda e, rc=rc: e.dma_start(
                        out=stgB[:, 0:2048] if False else wdcache[:, 0:DFF1],
                        in_=V1_i[rc * 128:(rc + 1) * 128, :]), inc=d_gp)
                    b.dma(SYN, lambda e, rc=rc: e.dma_start(
                        out=bass.AP(v1b_d, rc * 128 * DFF1, [[DFF1, 128], [1, DFF1]]),
                        in_=wdcache[:, 0:DFF1]), waits=[gt], inc=d_in)
                    b.waits(GPS, [(d_in, d_in.n)])
                for fc in range(8):
                    gt = b.dma(GPS, lambda e, fc=fc: e.dma_start(
                        out=wdcache[:, 0:1024],
                        in_=bass.AP(U2_i.tensor, fc * 512 * 256,
                                    [[1024, 128], [1, 1024]])), inc=d_gp)
                    b.dma(SYN, lambda e, fc=fc: e.dma_start(
                        out=bass.AP(u2b_d, fc * 512 * 256, [[1024, 128], [1, 1024]]),
                        in_=wdcache[:, 0:1024]), waits=[gt], inc=d_in)
                    b.waits(GPS, [(d_in, d_in.n)])
            b.barrier()
            Hc = kTc
            QTb = opre
            for tg in range(4):
                for rM in range(2):
                    pP = psum[rM]
                    for j in range(NPT):
                        b.op(TEN, lambda e, j=j, rM=rM, tg=tg, pP=pP: e.matmul(
                            pP[:], wstgB[:, j * 256 + rM * 128:j * 256 + (rM + 1) * 128],
                            uT[:, j * T + tg * 512:j * T + (tg + 1) * 512],
                            start=(j == 0), stop=(j == NPT - 1)),
                            waits=[(ct[VEC], ct[VEC].n), (ct[SCA], ct[SCA].n)]
                            if j == 0 else ())
                    eng = VEC if rM == 0 else SCA
                    if eng == VEC:
                        b.op(VEC, lambda e, rM=rM, pP=pP: e.tensor_copy(
                            PTb[:, rM * 512:(rM + 1) * 512], pP[:]),
                            waits=[(ct[TEN], ct[TEN].n)])
                    else:
                        b.op(SCA, lambda e, rM=rM, pP=pP: e.copy(
                            PTb[:, rM * 512:(rM + 1) * 512], pP[:]),
                            waits=[(ct[TEN], ct[TEN].n)])
                ptok = [(ct[VEC], ct[VEC].n), (ct[SCA], ct[SCA].n)]
                for fc in range(32):
                    sl = fc % 2
                    wb_, dw = (wbufA, d_wA) if sl == 0 else (wbufB, d_wB)
                    wf = wfree[sl]
                    for rc in range(2):
                        b.dma(SYN, lambda e, fc=fc, rc=rc, wb_=wb_: e.dma_start(
                            out=wb_[:, rc * 128:(rc + 1) * 128],
                            in_=bass.AP(v1b_d, rc * 128 * DFF1 + fc * 128,
                                        [[DFF1, 128], [1, 128]])),
                            waits=[(wf, wf.n)], inc=dw)
                    b.dma(SYN, lambda e, fc=fc, wb_=wb_: e.dma_start(
                        out=wb_[:, 256:512],
                        in_=bass.AP(u2b_d, fc * 128 * 256, [[256, 128], [1, 256]])),
                        waits=[(wf, wf.n)], inc=dw)
                    pH = psum[2 + sl]
                    for rc in range(2):
                        b.op(TEN, lambda e, rc=rc, fc=fc, pH=pH, wb_=wb_, dw=dw: e.matmul(
                            pH[:], wb_[:, rc * 128:(rc + 1) * 128],
                            PTb[:, rc * 512:(rc + 1) * 512],
                            start=(rc == 0), stop=(rc == 1)),
                            waits=([(dw, dw.n)] + ptok + [(ct[SCA], ct[SCA].n)])
                            if rc == 0 else ())
                    st = b.op(SCA, lambda e, fc=fc, sl=sl, pH=pH: e.activation(
                        Hc[:, sl * 512:(sl + 1) * 512], pH[:], AF.Gelu_apprx_tanh,
                        bias=stg2[:, 32 + fc:33 + fc], scale=1.0),
                        waits=[(ct[TEN], ct[TEN].n)])
                    for rM in range(2):
                        pQ = psum[4 + rM]
                        b.op(TEN, lambda e, rM=rM, fc=fc, sl=sl, pQ=pQ, wb_=wb_: e.matmul(
                            pQ[:], wb_[:, 256 + rM * 128:256 + (rM + 1) * 128],
                            Hc[:, sl * 512:(sl + 1) * 512],
                            start=(fc == 0), stop=(fc == 31)),
                            waits=[st] if rM == 0 else ())
                    b.op(TEN, lambda e: e.nop(), inc=wf)
                qtok0 = b.op(VEC, lambda e: e.tensor_copy(QTb[:, 0:512], psum[4][:]),
                             waits=[(ct[TEN], ct[TEN].n)])
                qtok1 = b.op(SCA, lambda e: e.copy(QTb[:, 512:1024], psum[5][:]),
                             waits=[(ct[TEN], ct[TEN].n)])
                for ti in range(4):
                    tt = tg * 4 + ti
                    for oc in range(2):
                        po = psum[6]
                        for rc in range(2):
                            b.op(TEN, lambda e, rc=rc, ti=ti, oc=oc, po=po: e.matmul(
                                po[:], QTb[:, rc * 512 + ti * 128:rc * 512 + (ti + 1) * 128],
                                stgB[:, rc * 1024 + oc * 512:rc * 1024 + (oc + 1) * 512],
                                start=(rc == 0), stop=False),
                                waits=[qtok0, qtok1, (ct[VEC], ct[VEC].n)]
                                if rc == 0 else ())
                        b.op(TEN, lambda e, oc=oc, po=po: e.matmul(
                            po[:], pcb[0:1, 0:128],
                            wstgB[0:1, 2048 + oc * 512:2048 + (oc + 1) * 512],
                            start=False, stop=True))
                        b.op(VEC, lambda e, tt=tt, oc=oc, po=po: e.scalar_tensor_tensor(
                            xres[:, tt * D + oc * 512:tt * D + (oc + 1) * 512],
                            po[:], 0.5,
                            xres[:, tt * D + oc * 512:tt * D + (oc + 1) * 512],
                            AL.mult, AL.add), waits=[(ct[TEN], ct[TEN].n)])
            b.barrier()

        EXO_K = 0
        EXO_EU = NH * 65 * T
        EXO_ED = EXO_EU + NPT * 128 * T
        EXO_M = EXO_ED + NPT * 128 * T
        SUPW = 4 * 128

        def emit_l1attn():
            emit_ln()
            b.barrier()
            emit_transpose(ub, uT)
            for t in range(NT):
                b.dma(SYN, lambda e, t=t: e.dma_start(
                    out=bass.AP(u_sp, t * 128 * D, [[D, 128], [1, D]]),
                    in_=ub[:, t * D:(t + 1) * D]),
                    waits=[(ct[SCA], ct[SCA].n), (ct[VEC], ct[VEC].n)], inc=d_in)
            b.barrier()
            b.dma(SYN, lambda e: e.dma_start(
                out=stats[:, 80:128],
                in_=bass.AP(battn.tensor, 0, [[1, 128], [1024, 6], [128, NPT]]),
                allow_slow_non_contiguous=True), inc=d_in)
            b.barrier()
            b.op(VEC, lambda e: e.tensor_scalar(
                stg2[:, 70:78], stats[:, 80:88], 0.125, None, AL.mult))
            b.dma(SYN, lambda e: e.dma_start(out=fek_sb[:, 0:512], in_=fek_up),
                  inc=d_in)
            b.dma(SYN, lambda e: e.dma_start(out=fek_sb[:, 512:1024], in_=fek_dn),
                  inc=d_in)
            for j in range(NPT):
                b.dma(GPS, lambda e, j=j: e.dma_start(
                    out=blkon_sb[:, j * NH:(j + 1) * NH], in_=blkones[j]),
                    inc=d_gp)
            b.barrier()

            wslotA = wdcache[:, 0:4096]
            wslotB = wdcache[:, 4096:8192]
            wAf = wfree[0]
            wBf = wfree[1]


            def proj_feat(wT_ap, bias_col, scale_col, dst_fn, sq_to=None,
                          extra=None):
                d_pj = [b.sem("d_pjA"), b.sem("d_pjB")]
                for half in range(2):
                    dw = d_pj[half]
                    wf = wfree[0] if half == 0 else wfree[1]
                    slot = wslotA if half == 0 else wslotB
                    for j in range(NPT):
                        b.dma(GPS, lambda e, j=j, half=half, slot=slot,
                              wT_ap=wT_ap: e.dma_start(
                            out=slot[:, j * 512:(j + 1) * 512],
                            in_=wT_ap[j * 128:(j + 1) * 128,
                                      half * 512:(half + 1) * 512]),
                            waits=[(wf, wf.n)], inc=dw)
                last_pq_rd = {0: None, 1: None}
                for half in range(2):
                    dw = d_pj[half]
                    slot = wslotA if half == 0 else wslotB
                    wf = wfree[0] if half == 0 else wfree[1]
                    for m in range(4):
                        for tg in range(4):
                            pq = psum[tg % 2]
                            for j in range(NPT):
                                b.op(TEN, lambda e, j=j, m=m, tg=tg, pq=pq,
                                     slot=slot: e.matmul(
                                    pq[:], slot[:, j * 512 + m * 128:
                                                j * 512 + (m + 1) * 128],
                                    uT[:, j * T + tg * 512:j * T + (tg + 1) * 512],
                                    start=(j == 0), stop=(j == NPT - 1)),
                                    waits=[(dw, dw.n)] + [t for t in
                                           (last_pq_rd[tg % 2],)
                                           if t is not None]
                                    if j == 0 else ())
                            mtot = half * 4 + m
                            stok = b.op(SCA, lambda e, pq=pq, mtot=mtot,
                                        bias_col=bias_col, scale_col=scale_col:
                                        e.activation(
                                stgB[:, 0:512], pq[:], AF.Identity,
                                bias=bias_col[:, mtot:mtot + 1] if bias_col is not None else 0.0,
                                scale=scale_col if scale_col is not None else 1.0),
                                waits=[(ct[TEN], ct[TEN].n)])
                            last_pq_rd[tg % 2] = stok
                            dst_fn(mtot, tg, stok)
                            if sq_to is not None:
                                sqt = b.op(SCA, lambda e: e.activation(
                                    stgB[:, 512:1024], stgB[:, 0:512], AF.Square,
                                    bias=0.0, scale=1.0))
                                pn = psum[3 + tg]
                                b.op(TEN, lambda e, mtot=mtot, pn=pn: e.matmul(
                                    pn[0:NH, 0:512], blkon_sb[:, mtot * NH:(mtot + 1) * NH],
                                    stgB[:, 512:1024],
                                    start=(mtot == 0), stop=(mtot == 7)),
                                    waits=[sqt])
                    if half == 1 and sq_to is not None:
                        for tg in range(4):
                            pn = psum[3 + tg]
                            vt = b.op(VEC, lambda e, pn=pn: e.tensor_copy(
                                stg2[0:NH, 0:512], pn[0:NH, 0:512]),
                                waits=[(ct[TEN], ct[TEN].n)])
                            b.dma(SYN, lambda e, tg=tg, sq_to=sq_to: e.dma_start(
                                out=bass.AP(nrm_d, sq_to * NH * T + tg * 512,
                                            [[T, NH], [1, 512]]),
                                in_=stg2[0:NH, 0:512]), waits=[vt], inc=d_in)
                            b.waits(VEC, [(d_in, d_in.n)])
                    b.op(TEN, lambda e: e.nop(), inc=wf)

            def q_dst(mtot, tg, stok):
                for h01 in range(2):
                    h = mtot * 2 + h01
                    b.dma(SYN, lambda e, h=h, h01=h01, tg=tg: e.dma_start(
                        out=bass.AP(qhat_d, ((tg * NH + h) * 65) * 512,
                                    [[512, 64], [1, 512]]),
                        in_=stgB[h01 * 64:(h01 + 1) * 64, 0:512]),
                        waits=[stok], inc=d_in)
                b.waits(SCA, [(d_in, d_in.n)])
            b.op(VEC, lambda e: e.memset(stg2[:, 79:80], 0.125))
            b.barrier()
            proj_feat(WqT, stg2[:, 70:78], stg2[:, 79:80], q_dst, sq_to=0)
            b.barrier()
            def eq_dst(mtot, tg, stok):
                b.dma(SYN, lambda e, mtot=mtot, tg=tg: e.dma_start(
                    out=bass.AP(eqT_d, ((tg * NPT + mtot) * 128) * 512,
                                [[512, 128], [1, 512]]),
                    in_=stgB[:, 0:512]), waits=[stok], inc=d_in)
                b.waits(SCA, [(d_in, d_in.n)])
            proj_feat(WeqT, stats[:, 112:120], None, eq_dst, sq_to=1)
            b.barrier()
            def k_dst(mtot, tg, stok):
                for h01 in range(2):
                    h = mtot * 2 + h01
                    b.dma(SYN, lambda e, h=h, h01=h01, tg=tg: e.dma_start(
                        out=bass.AP(ex_in, EXO_K + (h * 4 + tg) * 65 * 512,
                                    [[512, 64], [1, 512]]),
                        in_=stgB[h01 * 64:(h01 + 1) * 64, 0:512]),
                        waits=[stok], inc=d_in)
                b.waits(SCA, [(d_in, d_in.n)])
            proj_feat(WkT, stats[:, 88:96], None, k_dst, sq_to=2)
            b.op(VEC, lambda e: e.memset(kTc[0:1, 0:512], 1.0))
            b.barrier()
            for h in range(NH):
                for sct in range(4):
                    b.dma(SYN, lambda e, h=h, sct=sct: e.dma_start(
                        out=bass.AP(ex_in,
                                    EXO_K + (h * 4 + sct) * 65 * 512 + 64 * 512,
                                    [[1, 1], [1, 512]]),
                        in_=kTc[0:1, 0:512]), inc=d_in)
            b.barrier()
            def ek_dst(mtot, tg, stok):
                v1 = b.op(VEC, lambda e: e.tensor_mul(
                    stgB[:, 1024:1536], stgB[:, 0:512],
                    fek_sb[:, 0:512]), waits=[stok])
                v2 = b.op(VEC, lambda e: e.tensor_mul(
                    stgB[:, 1536:2048], stgB[:, 0:512],
                    fek_sb[:, 512:1024]))
                b.dma(SYN, lambda e, mtot=mtot, tg=tg: e.dma_start(
                    out=bass.AP(ex_in, EXO_EU + (mtot * 4 + tg) * 128 * 512,
                                [[512, 128], [1, 512]]),
                    in_=stgB[:, 1024:1536]), waits=[v1], inc=d_in)
                b.dma(SYN, lambda e, mtot=mtot, tg=tg: e.dma_start(
                    out=bass.AP(ex_in, EXO_ED + (mtot * 4 + tg) * 128 * 512,
                                [[512, 128], [1, 512]]),
                    in_=stgB[:, 1536:2048]), waits=[v2], inc=d_in)
                b.waits(VEC, [(d_in, d_in.n)])
            proj_feat(WekT, stats[:, 120:128], None, ek_dst, sq_to=3)
            b.barrier()
            for r_, col in ((2, 0), (3, 1)):
                for hh in range(2):
                    lt = b.dma(SYN, lambda e, r_=r_, hh=hh: e.dma_start(
                        out=stg2[0:NH, 0:1024],
                        in_=bass.AP(nrm_d, r_ * NH * T + hh * 1024,
                                    [[T, NH], [1, 1024]])),
                        waits=[(ct[VEC], ct[VEC].n)], inc=d_in)
                    b.op(VEC, lambda e, hh=hh: e.reduce_max(
                        stg[0:NH, hh:hh + 1], stg2[0:NH, 0:1024], axis=AX),
                        waits=[lt])
                b.op(VEC, lambda e: e.tensor_max(
                    stg[0:NH, 2:3], stg[0:NH, 0:1], stg[0:NH, 1:2]))
                st = b.op(SCA, lambda e: e.activation(
                    stg[0:NH, 3:4], stg[0:NH, 2:3], AF.Sqrt,
                    bias=0.0, scale=1.0), waits=[(ct[VEC], ct[VEC].n)])
                vt = b.op(VEC, lambda e: e.tensor_scalar(
                    stg[0:NH, 4:5], stg[0:NH, 3:4],
                    1.001, None, AL.mult), waits=[st])
                b.dma(SYN, lambda e, col=col: e.dma_start(
                    out=bass.AP(sx_in, col * NH, [[1, 1], [1, NH]]),
                    in_=stg[0:NH, 4:5]), waits=[vt], inc=d_in)
            b.barrier()
            cct1 = b.op(GPS, lambda e: e.collective_compute(
                "AllGather", AL.bypass,
                replica_groups=[[0, 1], [2, 3], [4, 5], [6, 7]],
                ins=[sx_in[:]], outs=[sx_out[:]]), inc=ccs)

            for j in range(NPT):
                b.dma(GPS, lambda e, j=j: e.dma_start(
                    out=wstgB[:, j * 32:(j + 1) * 32],
                    in_=apw[j * 128:(j + 1) * 128, :]), inc=d_gp)
            b.dma(GPS, lambda e: e.dma_start(
                out=wstgB[0:1, 256:288], in_=apb), inc=d_gp)
            b.op(VEC, lambda e: e.memset(pcb[0:1, 0:128], 1.0))
            b.barrier()
            for tt in range(NT):
                pa = psum[tt % 2]
                for j in range(NPT):
                    b.op(TEN, lambda e, j=j, tt=tt, pa=pa: e.matmul(
                        pa[:, 0:32], uT[:, j * T + tt * 128:j * T + (tt + 1) * 128],
                        wstgB[:, j * 32:(j + 1) * 32],
                        start=(j == 0), stop=False, skip_group_check=True),
                        waits=[(ct[VEC], ct[VEC].n), (ct[SCA], ct[SCA].n)]
                        if j == 0 else ())
                b.op(TEN, lambda e, tt=tt, pa=pa: e.matmul(
                    pa[:, 0:32], pcb[0:1, 0:128], wstgB[0:1, 256:288],
                    start=False, stop=True, skip_group_check=True))
                s1 = b.op(SCA, lambda e, pa=pa: e.activation(
                    stg2[:, 0:16], pa[:, 0:16], AF.Sigmoid, bias=0.0, scale=1.0),
                    waits=[(ct[TEN], ct[TEN].n)])
                s2 = b.op(SCA, lambda e, pa=pa: e.activation(
                    stg2[:, 16:32], pa[:, 16:32], AF.Tanh, bias=0.0, scale=1.0))
                v1 = b.op(VEC, lambda e: e.tensor_scalar(
                    stg[:, 0:16], stg2[:, 16:32], math.pi, None, AL.mult),
                    waits=[s2])
                s3 = b.op(SCA, lambda e: e.activation(
                    stg[:, 16:32], stg[:, 0:16], AF.Sin, bias=0.0, scale=1.0),
                    waits=[v1])
                s4 = b.op(SCA, lambda e: e.activation(
                    stg[:, 32:48], stg[:, 0:16], AF.Sin, bias=0.0,
                    scale=0.5), waits=[v1])
                vm1 = b.op(VEC, lambda e: e.tensor_mul(
                    stg[:, 48:64], stg[:, 32:48], stg[:, 32:48]), waits=[s4])
                b.op(VEC, lambda e: e.tensor_scalar(
                    stg[:, 48:64], stg[:, 48:64], -2.0, 1.0, AL.mult, AL.add))
                v3 = b.op(VEC, lambda e: e.tensor_scalar(
                    stg[:, 64:80], stg2[:, 0:16], SUPC, None, AL.mult),
                    waits=[s1])
                b.op(VEC, lambda e, tt=tt: e.tensor_mul(
                    c12_sb[:, tt * 32:tt * 32 + 16], stg[:, 64:80], stg[:, 48:64]))
                b.op(VEC, lambda e, tt=tt: e.tensor_mul(
                    c12_sb[:, tt * 32 + 16:tt * 32 + 32], stg[:, 64:80],
                    stg[:, 16:32]))
            b.barrier()
            for h in range(NH):
                gt = b.dma(GPS, lambda e, h=h: e.dma_start(
                    out=stgB[0:64, 0:D], in_=wv_rows[h * 64:(h + 1) * 64, :]),
                    waits=[(ct[VEC], ct[VEC].n)], inc=d_gp)
                gt2 = b.dma(GPS, lambda e, h=h: e.dma_start(
                    out=stgB[0:64, 1024:1088], in_=mixwT[h]), inc=d_gp)
                for j in range(NPT):
                    pw = psum[j % 2]
                    b.op(TEN, lambda e, j=j, pw=pw: e.matmul(
                        pw[:, 0:64], stgB[0:64, j * 128:(j + 1) * 128],
                        stgB[0:64, 1024:1088], start=True, stop=True,
                        skip_group_check=True),
                        waits=[gt2, (ct[VEC], ct[VEC].n)] if j == 0
                        else [(ct[VEC], ct[VEC].n)])
                    vt = b.op(VEC, lambda e, j=j, pw=pw: e.tensor_copy(
                        stgB[:, 1152 + (j % 4) * 64:1152 + (j % 4) * 64 + 64],
                        pw[:, 0:64]), waits=[(ct[TEN], ct[TEN].n)])
                    b.dma(SYN, lambda e, h=h, j=j: e.dma_start(
                        out=bass.AP(wsup_d, (j * 128) * 1024 + h * 64,
                                    [[1024, 128], [1, 64]]),
                        in_=stgB[:, 1152 + (j % 4) * 64:1152 + (j % 4) * 64 + 64]),
                        waits=[vt], inc=d_in)
                    if j % 4 == 3:
                        b.waits(VEC, [(d_in, d_in.n)])
            b.barrier()
            b.dma(GPS, lambda e: e.dma_start(
                out=wstgB[0:1, 512:1536], in_=mixbr), inc=d_gp)
            b.dma(GPS, lambda e: e.dma_start(
                out=wstgB[0:1, 1536:2560], in_=battn[2:3, :]), inc=d_gp)
            for half in range(2):
                for j in range(NPT):
                    b.dma(GPS, lambda e, j=j, half=half: e.dma_start(
                        out=wslotA[:, j * 512:(j + 1) * 512],
                        in_=WvT[j * 128:(j + 1) * 128,
                                half * 512:(half + 1) * 512]),
                        waits=[(ct[TEN], ct[TEN].n)], inc=d_gp)
                for j in range(NPT):
                    b.dma(SYN, lambda e, j=j, half=half: e.dma_start(
                        out=wslotB[:, j * 512:(j + 1) * 512],
                        in_=bass.AP(wsup_d, (j * 128) * 1024 + half * 512,
                                    [[1024, 128], [1, 512]])), inc=d_in)
                b.barrier()
                for tt in range(NT):
                    pv = psum[0 + tt % 2]
                    psu = psum[2 + tt % 2]
                    for j in range(NPT):
                        b.op(TEN, lambda e, j=j, tt=tt, pv=pv: e.matmul(
                            pv[:], uT[:, j * T + tt * 128:j * T + (tt + 1) * 128],
                            wslotA[:, j * 512:(j + 1) * 512],
                            start=(j == 0), stop=False, skip_group_check=True),
                            waits=[(ct[VEC], ct[VEC].n), (ct[SCA], ct[SCA].n)]
                            if j == 0 else ())
                        b.op(TEN, lambda e, j=j, tt=tt, psu=psu: e.matmul(
                            psu[:], uT[:, j * T + tt * 128:j * T + (tt + 1) * 128],
                            wslotB[:, j * 512:(j + 1) * 512],
                            start=(j == 0), stop=False, skip_group_check=True))
                    b.op(TEN, lambda e, half=half, pv=pv: e.matmul(
                        pv[:], pcb[0:1, 0:128],
                        wstgB[0:1, 1536 + half * 512:2048 + half * 512],
                        start=False, stop=True, skip_group_check=True))
                    b.op(TEN, lambda e, half=half, psu=psu: e.matmul(
                        psu[:], pcb[0:1, 0:128],
                        wstgB[0:1, 512 + half * 512:1024 + half * 512],
                        start=False, stop=True, skip_group_check=True))
                    mmt = (ct[TEN], ct[TEN].n)
                    vms = b.op(VEC, lambda e: e.memset(mix_sb[:], 1.0), waits=[mmt])
                    b.op(VEC, lambda e, pv=pv: e.tensor_copy(vc[:, 0:512], pv[:]))
                    for h01 in range(8):
                        h = half * 8 + h01
                        c1 = c12_sb[:, tt * 32 + h:tt * 32 + h + 1]
                        c2 = c12_sb[:, tt * 32 + 16 + h:tt * 32 + 17 + h]
                        b.op(VEC, lambda e, h01=h01, psu=psu, c1=c1:
                             e.scalar_tensor_tensor(
                            mix_sb[:, h01 * 65:h01 * 65 + 64],
                            psu[:, h01 * 64:(h01 + 1) * 64], c1,
                            vc[:, h01 * 64:(h01 + 1) * 64], AL.mult, AL.add))
                        b.op(VEC, lambda e, h01=h01, psu=psu, c2=c2:
                             e.scalar_tensor_tensor(
                            mix_sb[:, h01 * 65 + 1:h01 * 65 + 64],
                            psu[:, h01 * 64:h01 * 64 + 63], c2,
                            mix_sb[:, h01 * 65 + 1:h01 * 65 + 64],
                            AL.mult, AL.add))
                        b.op(VEC, lambda e, h01=h01, psu=psu, c2=c2:
                             e.scalar_tensor_tensor(
                            mix_sb[:, h01 * 65:h01 * 65 + 1],
                            psu[:, h01 * 64 + 63:h01 * 64 + 64], c2,
                            mix_sb[:, h01 * 65:h01 * 65 + 1],
                            AL.mult, AL.add))
                    for hp01 in range(4):
                        hp = half * 4 + hp01
                        sup_i, tile_i = tt // 4, tt % 4
                        b.dma(SYN, lambda e, hp=hp, hp01=hp01, sup_i=sup_i,
                              tile_i=tile_i: e.dma_start(
                            out=bass.AP(ex_in,
                                        EXO_M + ((hp * 4 + sup_i) * 128 * 4
                                                 + tile_i) * 130,
                                        [[4 * 130, 128], [1, 130]]),
                            in_=mix_sb[:, hp01 * 130:(hp01 + 1) * 130]),
                            waits=[(ct[VEC], ct[VEC].n)], inc=d_in)
                    b.waits(VEC, [(d_in, d_in.n)])
                b.barrier()
            cct2 = b.op(GPS, lambda e: e.collective_compute(
                "AllGather", AL.bypass,
                replica_groups=[[0, 1], [2, 3], [4, 5], [6, 7]],
                ins=[ex_in[:]], outs=[ex_out[:]]), inc=ccs)
            b.waits(GPS, [cct1])
            gtx = b.dma(GPS, lambda e: e.dma_start(
                out=srow[0:1, 128:192],
                in_=bass.AP(sx_out, 0, [[1, 1], [1, 64]])), inc=d_gp)
            b.waits(VEC, [gtx])
            b.op(VEC, lambda e: e.tensor_max(srow[0:1, 192:224],
                                             srow[0:1, 128:160],
                                             srow[0:1, 160:192]))
            dt = b.dma(SYN, lambda e: e.dma_start(
                out=bass.AP(scl_d, 2 * DFF0 + 64, [[1, 1], [1, 32]]),
                in_=srow[0:1, 192:224]), waits=[(ct[VEC], ct[VEC].n)], inc=d_in)
            dt2 = b.dma(SYN, lambda e: e.dma_start(
                out=stg2[0:NH, 1020:1022],
                in_=bass.AP(scl_d, 2 * DFF0 + 64, [[1, NH], [NH, 2]]),
                allow_slow_non_contiguous=True), waits=[dt], inc=d_in)
            for g in range(4):
                lt0 = b.dma(SYN, lambda e, g=g: e.dma_start(
                    out=stg2[0:NH, 0:512],
                    in_=bass.AP(nrm_d, 0 * NH * T + g * 512, [[T, NH], [1, 512]])),
                    waits=[dt2, (ct[VEC], ct[VEC].n), (ct[SCA], ct[SCA].n)],
                    inc=d_in)
                lt1 = b.dma(SYN, lambda e, g=g: e.dma_start(
                    out=stg[0:NH, 0:512],
                    in_=bass.AP(nrm_d, 1 * NH * T + g * 512, [[T, NH], [1, 512]])),
                    waits=[(ct[VEC], ct[VEC].n), (ct[SCA], ct[SCA].n)], inc=d_in)
                s1 = b.op(SCA, lambda e: e.activation(
                    stg2[0:NH, 0:512], stg2[0:NH, 0:512], AF.Sqrt,
                    bias=0.0, scale=1.0), waits=[lt1])
                s2 = b.op(SCA, lambda e: e.activation(
                    stg[0:NH, 0:512], stg[0:NH, 0:512], AF.Sqrt,
                    bias=0.0, scale=1.0), waits=[lt1])
                v1 = b.op(VEC, lambda e: e.tensor_scalar(
                    stg2[0:NH, 0:512], stg2[0:NH, 0:512], stg2[0:NH, 1020:1021],
                    None, AL.mult), waits=[s1, s2])
                v2 = b.op(VEC, lambda e: e.tensor_scalar(
                    stg[0:NH, 0:512], stg[0:NH, 0:512],
                    stg2[0:NH, 1021:1022], 1.0 / 16.0, AL.mult, AL.mult))
                v3 = b.op(VEC, lambda e: e.tensor_add(
                    stg2[0:NH, 0:512], stg2[0:NH, 0:512], stg[0:NH, 0:512]))
                v4 = b.op(VEC, lambda e: e.tensor_scalar(
                    stgB[0:NH, 0:512], stg2[0:NH, 0:512], -1.0, BSHIFT,
                    AL.mult, AL.add))
                for h in range(NH):
                    b.dma(SYN, lambda e, g=g, h=h: e.dma_start(
                        out=bass.AP(qhat_d, ((g * NH + h) * 65 + 64) * 512,
                                    [[1, 1], [1, 512]]),
                        in_=stgB[h:h + 1, 0:512]), waits=[v4], inc=d_in)
                b.waits(VEC, [(d_in, d_in.n)])
                b.waits(SCA, [(d_in, d_in.n)])
            b.barrier()
            b.waits(GPS, [cct2])
            b.barrier()

        OFFS = [0, 1, 2, 3, 4, -3, -2, -1]

        def emit_l1attn_inner():
            eqdn_g = uT[:, 0:4096]
            equp_g = uT[:, 4096:8192]
            kh_s = [uT[:, 8192:9216], uT[:, 9216:10240]]
            ekup_s = [uT[:, 10240:10752], uT[:, 10752:11264]]
            ekdn_s = [uT[:, 11264:11776], uT[:, 11776:12288]]
            mixd_s = [uT[:, 12288:12808], uT[:, 12808:13328]]
            qh_s = uT[:, 13328:14352]
            pT_s = [uT[:, 14352:14864], uT[:, 14864:15376]]
            pT_b = [qTc[:, 0:512], qTc[:, 512:1024]]
            ss_s = [uT[:, 15376:15504], uT[:, 15504:15632]]
            eqld = uT[:, 15632:16144]
            ctxT = ub
            b.dma(SYN, lambda e: e.dma_start(out=fq_sb[:, 0:512], in_=fq_dn),
                  inc=d_in)
            b.dma(SYN, lambda e: e.dma_start(out=fq_sb[:, 512:1024], in_=fq_up),
                  inc=d_in)
            b.dma(SYN, lambda e: e.dma_start(out=tdg_sb[:], in_=tdiag_c), inc=d_in)
            b.dma(SYN, lambda e: e.dma_start(out=escm_sb[:], in_=escm_in), inc=d_in)
            b.barrier()
            CH = 2048
            NCOL = EX_TOT // 128
            wslotA = wdcache[:, 0:4096]
            wslotB = wdcache[:, 4096:8192]
            d_bls = [b.sem("d_bls0"), b.sem("d_bls1")]
            sprev = [None, None]
            nch = (NCOL + CH - 1) // CH
            for ib in range(nch):
                c0 = ib * CH
                cw = min(CH, NCOL - c0)
                base = c0 * 128
                slb = ib % 2
                bufA = wslotA[:, slb * 2048:slb * 2048 + cw]
                bufB = wslotB[:, slb * 2048:slb * 2048 + cw]
                obuf = (stgB if slb == 0 else wstgB)[:, 0:cw]
                lw = [sprev[slb]] if sprev[slb] is not None else []
                b.dma(SYN, lambda e, bufA=bufA, base=base, cw=cw: e.dma_start(
                    out=bufA, in_=bass.AP(ex_out, base, [[cw, 128], [1, cw]])),
                    waits=lw, inc=d_in)
                lt = b.dma(SYN, lambda e, bufB=bufB, base=base, cw=cw: e.dma_start(
                    out=bufB,
                    in_=bass.AP(ex_out, EX_TOT + base, [[cw, 128], [1, cw]])),
                    inc=d_in)
                b.op(VEC, lambda e, bufA=bufA, obuf=obuf: e.tensor_scalar(
                    obuf, bufA, escm_sb[:, 32:33], None, AL.mult),
                    waits=[lt])
                vt = b.op(VEC, lambda e, bufB=bufB, obuf=obuf:
                          e.scalar_tensor_tensor(
                    obuf, bufB, escm_sb[:, 33:34], obuf, AL.mult, AL.add))
                sprev[slb] = b.dma(SYN, lambda e, base=base, cw=cw, obuf=obuf:
                                   e.dma_start(
                    out=bass.AP(exr_d, base, [[cw, 128], [1, cw]]),
                    in_=obuf), waits=[vt], inc=d_bls[slb])
            b.barrier()
            b.op(VEC, lambda e: e.memset(qh_s[64:96, 0:1024], 0.0))
            b.op(VEC, lambda e: e.memset(kh_s[0][64:96, 0:1024], 0.0))
            b.op(VEC, lambda e: e.memset(kh_s[1][64:96, 0:1024], 0.0))
            sfree = [b.sem("sf0"), b.sem("sf1")]
            d_s0 = b.sem("d_s0")
            d_s1 = b.sem("d_s1")
            d_t0 = b.sem("d_t0")
            d_t1 = b.sem("d_t1")
            d_u0 = b.sem("d_u0")
            d_u1 = b.sem("d_u1")
            d_q = b.sem("d_q")
            dsl = [[d_s0, d_s1], [d_t0, d_t1], [d_u0, d_u1]]

            def ex_src(reg, blk, rs, tile, shp):
                if rs < 4:
                    return bass.AP(ex_in, reg + (blk * 4 + rs) * tile, shp)
                return bass.AP(exr_d, reg + (blk * 4 + rs - 4) * tile, shp)

            last_exp = {0: None, 1: None}
            last_dadd = {0: None, 1: None}
            last_vt1 = [None]
            last_pc = {0: None, 1: None}
            scnt = 0

            def emit_pv(a):
                sl_, sub_, h01_, pc_, first_, last_, etok_, pT_ = a
                b.op(TEN, lambda e, sl=sl_, sub=sub_, h01=h01_, pc=pc_,
                     first=first_, last=last_, pT=pT_: e.matmul(
                    pc[0:65, 0:512],
                    mixd_s[sl][:, sub * 130 + h01 * 65:
                               sub * 130 + (h01 + 1) * 65],
                    pT[:, 0:512],
                    start=first, stop=last, skip_group_check=True),
                    waits=[etok_])
            for g in range(4):
                for j in range(NPT):
                    lt = b.dma(SYN, lambda e, g=g, j=j: e.dma_start(
                        out=eqld[:, 0:512],
                        in_=bass.AP(eqT_d, ((g * NPT + j) * 128) * 512,
                                    [[512, 128], [1, 512]])),
                        waits=[(ct[VEC], ct[VEC].n), (ct[TEN], ct[TEN].n)],
                        inc=d_in)
                    b.op(VEC, lambda e, j=j: e.tensor_mul(
                        eqdn_g[:, j * 512:(j + 1) * 512], eqld[:, 0:512],
                        fq_sb[:, 0:512]), waits=[lt])
                    b.op(VEC, lambda e, j=j: e.tensor_mul(
                        equp_g[:, j * 512:(j + 1) * 512], eqld[:, 0:512],
                        fq_sb[:, 512:1024]))
                b.barrier()
                for hp in range(8):
                    qlt = b.dma(SYN, lambda e, g=g, hp=hp: e.dma_start(
                        out=qh_s[0:65, 0:1024],
                        in_=bass.AP(qhat_d, ((g * NH + 2 * hp) * 65) * 512,
                                    [[512, 65], [65 * 512, 2], [1, 512]])),
                        waits=[(ct[TEN], ct[TEN].n)], inc=d_q)
                    pctx = [psum[3], psum[4]]
                    first_ctx = [True, True]
                    pv_pend = None
                    for bsi, OFF in enumerate(OFFS):
                        sl = bsi % 2
                        rs = (g + OFF) % 8
                        src = ex_src(EXO_K, 2 * hp, rs, 33280,
                                     [[512, 65], [4 * 33280, 2], [1, 512]])
                        b.dma(SYN, lambda e, src=src, sl=sl: e.dma_start(
                            out=kh_s[sl][0:65, 0:1024], in_=src),
                            waits=[(sfree[sl], sfree[sl].n)], inc=dsl[0][sl])
                        src = ex_src(EXO_M, hp, rs, 66560, [[520, 128], [1, 520]])
                        b.dma(SYN, lambda e, src=src, sl=sl: e.dma_start(
                            out=mixd_s[sl][:, 0:520], in_=src),
                            waits=[(sfree[sl], sfree[sl].n)], inc=dsl[0][sl])
                        src = ex_src(EXO_EU, hp, rs, 65536, [[512, 128], [1, 512]])
                        b.dma(SCA, lambda e, src=src, sl=sl: e.dma_start(
                            out=ekup_s[sl][:, 0:512], in_=src),
                            waits=[(sfree[sl], sfree[sl].n)], inc=dsl[1][sl])
                        src = ex_src(EXO_ED, hp, rs, 65536, [[512, 128], [1, 512]])
                        b.dma(GPS, lambda e, src=src, sl=sl: e.dma_start(
                            out=ekdn_s[sl][:, 0:512], in_=src),
                            waits=[(sfree[sl], sfree[sl].n)], inc=dsl[2][sl])
                        ldtok = [(dsl[0][sl], dsl[0][sl].n),
                                 (dsl[1][sl], dsl[1][sl].n),
                                 (dsl[2][sl], dsl[2][sl].n), (d_q, d_q.n)]
                        for h01 in range(2):
                            hb = h01 * 64
                            for sub in range(4):
                                r = 4 * OFF + sub
                                zb = (bsi * 4 + sub) % 2
                                pz = psum[zb]
                                pieces = []
                                if 0 <= r <= 3:
                                    if r < 3:
                                        pieces.append(("L", (r + 1) * 128, 512,
                                                       math.exp(12.8 * r)))
                                    if r > 0:
                                        pieces.append(("R", 0, r * 128,
                                                       math.exp(-12.8 * r)))
                                    pieces.append(("D", r * 128, (r + 1) * 128, 1.0))
                                elif -8 <= r <= -1:
                                    pieces.append(("L", 0, 512, math.exp(12.8 * r)))
                                elif 4 <= r <= 12:
                                    pieces.append(("R", 0, 512, math.exp(-12.8 * r)))
                                n_mm = 1 + len([p for p in pieces if p[0] != "D"])
                                b.op(TEN, lambda e, sl=sl, h01=h01, sub=sub, pz=pz:
                                     e.matmul(
                                    pz[:], kh_s[sl][0:96,
                                                    h01 * 512 + sub * 128:
                                                    h01 * 512 + (sub + 1) * 128],
                                    qh_s[0:96, h01 * 512:(h01 + 1) * 512],
                                    start=True, stop=(n_mm == 1),
                                    skip_group_check=True),
                                    waits=ldtok + [t for t in
                                                   (last_exp[zb], last_dadd[zb])
                                                   if t is not None])
                                mmi = 1
                                dpend = None
                                for kind, c0, c1, scl in pieces:
                                    if kind == "D":
                                        pE = psum[2]
                                        b.op(TEN, lambda e, sl=sl, hb=hb, sub=sub,
                                             g=g, hp=hp, pE=pE, c0=c0, c1=c1:
                                             e.matmul(
                                            pE[:, 0:128],
                                            ekup_s[sl][hb:hb + 64,
                                                       sub * 128:(sub + 1) * 128],
                                            eqdn_g[hb:hb + 64,
                                                   hp * 512 + c0:hp * 512 + c1],
                                            start=True, stop=True,
                                            skip_group_check=True),
                                            waits=[t for t in (last_vt1[0],)
                                                   if t is not None])
                                        vt1 = b.op(VEC, lambda e, pE=pE, c0=c0:
                                                   e.tensor_mul(
                                            stg[:, 0:128], pE[:, 0:128], tdg_sb[:]),
                                            waits=[(ct[TEN], ct[TEN].n)])
                                        last_vt1[0] = vt1
                                        dpend = (c0, vt1)
                                    else:
                                        base = eqdn_g if kind == "L" else equp_g
                                        eks = ekup_s if kind == "L" else ekdn_s
                                        if scl == 1.0:
                                            lhs = eks[sl][hb:hb + 64,
                                                          sub * 128:(sub + 1) * 128]
                                            sstok = ()
                                        else:
                                            sslot = mmi % 2
                                            ssl = ss_s[sslot]
                                            mcol = g * 8 + bsi
                                            b.op(VEC, lambda e, sl=sl, hb=hb,
                                                 sub=sub, scl=scl, ssl=ssl,
                                                 eks=eks, mcol=mcol:
                                                 e.tensor_scalar(
                                                ssl[hb:hb + 64, 0:128],
                                                eks[sl][hb:hb + 64,
                                                        sub * 128:(sub + 1) * 128],
                                                scl, escm_sb[hb:hb + 64,
                                                             mcol:mcol + 1],
                                                AL.mult, AL.mult),
                                                waits=ldtok + [t for t in
                                                       (last_pc[sslot],)
                                                       if t is not None])
                                            lhs = ssl[hb:hb + 64, 0:128]
                                            sstok = [(ct[VEC], ct[VEC].n)]
                                        mmi += 1
                                        pctok = b.op(TEN, lambda e, lhs=lhs,
                                             hb=hb, g=g,
                                             hp=hp, c0=c0, c1=c1, pz=pz, base=base,
                                             stop=(mmi == n_mm): e.matmul(
                                            pz[:, c0:c1], lhs,
                                            base[hb:hb + 64,
                                                 hp * 512 + c0:hp * 512 + c1],
                                            start=False, stop=stop,
                                            skip_group_check=True),
                                            waits=list(sstok))
                                        if scl != 1.0:
                                            last_pc[sslot] = pctok
                                dadd = None
                                if dpend is not None:
                                    c0, vt1 = dpend
                                    dadd = b.op(VEC, lambda e, pz=pz, c0=c0:
                                                e.tensor_add(
                                        pz[:, c0:c0 + 128], pz[:, c0:c0 + 128],
                                        stg[:, 0:128]),
                                        waits=[(ct[TEN], ct[TEN].n)])
                                    last_dadd[zb] = dadd
                                pT = pT_s[sl] if (scnt & 1) == 0 else pT_b[sl]
                                scnt += 1
                                etok = b.op(SCA, lambda e, pz=pz, pT=pT:
                                            e.activation(
                                    pT[:, 0:512], pz[:], AF.Exp,
                                    bias=0.0, scale=1.0),
                                    waits=[(ct[TEN], ct[TEN].n)]
                                    + ([dadd] if dadd is not None else []))
                                last_exp[zb] = etok
                                if pv_pend is not None:
                                    emit_pv(pv_pend)
                                pv_pend = (sl, sub, h01, pctx[h01],
                                           first_ctx[h01],
                                           (bsi == 7 and sub == 3), etok, pT)
                                first_ctx[h01] = False
                        if pv_pend is not None:
                            emit_pv(pv_pend)
                            pv_pend = None
                        b.op(TEN, lambda e: e.nop(), inc=sfree[sl])
                    for h01 in range(2):
                        pc = pctx[h01]
                        if h01 == 0:
                            vt = b.op(VEC, lambda e, pc=pc, hp=hp, g=g: e.tensor_copy(
                                ctxT[0:64, hp * T + g * 512:hp * T + (g + 1) * 512],
                                pc[0:64, 0:512]), waits=[(ct[TEN], ct[TEN].n)])
                        else:
                            vt = b.op(VEC, lambda e, pc=pc: e.tensor_copy(
                                stgB[0:64, 0:512], pc[0:64, 0:512]),
                                waits=[(ct[TEN], ct[TEN].n)])
                            b.dma(SYN, lambda e, hp=hp, g=g: e.dma_start(
                                out=ctxT[64:128, hp * T + g * 512:
                                         hp * T + (g + 1) * 512],
                                in_=stgB[0:64, 0:512]),
                                waits=[vt], inc=d_in)
                        vt2 = b.op(VEC, lambda e, pc=pc, h01=h01: e.tensor_copy(
                            stg2[64:65, h01 * 512:(h01 + 1) * 512],
                            pc[64:65, 0:512]))
                        b.dma(SYN, lambda e, hp=hp, g=g, h01=h01: e.dma_start(
                            out=bass.AP(den_d, (2 * hp + h01) * T + g * 512,
                                        [[1, 1], [1, 512]]),
                            in_=stg2[64:65, h01 * 512:(h01 + 1) * 512]),
                            waits=[vt2], inc=d_in)
                    b.waits(VEC, [(d_in, d_in.n)])
            b.barrier()

        def emit_l1attn_out():
            ctxT = ub
            for j in range(NPT):
                for g in range(4):
                    lt = b.dma(SYN, lambda e, j=j, g=g: e.dma_start(
                        out=bcst[:, 0:512],
                        in_=bass.AP(den_d, (2 * j) * T + g * 512,
                                    [[T, 2], [0, 64], [1, 512]])),
                        waits=[(ct[VEC], ct[VEC].n)], inc=d_in)
                    vt = b.op(VEC, lambda e: e.reciprocal(
                        bcst[:, 512:1024], bcst[:, 0:512]), waits=[lt])
                    b.op(VEC, lambda e, j=j, g=g: e.tensor_mul(
                        ctxT[:, j * T + g * 512:j * T + (g + 1) * 512],
                        ctxT[:, j * T + g * 512:j * T + (g + 1) * 512],
                        bcst[:, 512:1024]))
            b.barrier()
            WoTb = wdcache[:, 0:8192]
            for j in range(NPT):
                b.dma(GPS, lambda e, j=j: e.dma_start(
                    out=WoTb[:, j * D:(j + 1) * D],
                    in_=WoT[j * 128:(j + 1) * 128, :]), inc=d_gp)
            b.dma(GPS, lambda e: e.dma_start(
                out=stgB[0:1, 0:1024], in_=battn[3:4, :]), inc=d_gp)
            b.dma(SYN, lambda e: e.dma_start(
                out=bcst[:, 0:D],
                in_=bass.AP(lnrows.tensor, 0, [[0, 128], [1, D]])), inc=d_in)
            b.dma(SYN, lambda e: e.dma_start(
                out=bcst[:, D:2 * D],
                in_=bass.AP(lnrows.tensor, D, [[0, 128], [1, D]])), inc=d_in)
            b.op(VEC, lambda e: e.memset(pcb[0:1, 0:128], 1.0))
            b.barrier()
            last_po_rd = {0: None, 1: None}
            for tt in range(NT):
                ult = b.dma(SYN, lambda e, tt=tt: e.dma_start(
                    out=stgB[:, 1024:2048],
                    in_=bass.AP(u_sp, tt * 128 * D, [[D, 128], [1, D]])),
                    waits=[(ct[VEC], ct[VEC].n)], inc=d_in)
                for oc in range(2):
                    po = psum[5 + tt % 2]
                    for j in range(NPT):
                        b.op(TEN, lambda e, j=j, tt=tt, oc=oc, po=po: e.matmul(
                            po[:], ctxT[:, j * T + tt * 128:j * T + (tt + 1) * 128],
                            WoTb[:, j * D + oc * 512:j * D + (oc + 1) * 512],
                            start=(j == 0), stop=False, skip_group_check=True),
                            waits=[t for t in (last_po_rd[tt % 2],)
                                   if t is not None] if j == 0 else ())
                    b.op(TEN, lambda e, oc=oc, po=po: e.matmul(
                        po[:], pcb[0:1, 0:128],
                        stgB[0:1, oc * 512:(oc + 1) * 512],
                        start=False, stop=True, skip_group_check=True))
                    last_po_rd[tt % 2] = b.op(VEC, lambda e, oc=oc, po=po:
                                              e.scalar_tensor_tensor(
                        stg[:, oc * 512:(oc + 1) * 512], po[:], 1.0,
                        stgB[:, 1024 + oc * 512:1024 + (oc + 1) * 512],
                        AL.mult, AL.add),
                        waits=[(ct[TEN], ct[TEN].n), ult])
                b.op(VEC, lambda e: e.bn_stats(stg2[:, 0:6], stg[:, 0:512]))
                b.op(VEC, lambda e: e.bn_stats(stg2[:, 6:12], stg[:, 512:1024]))
                b.op(VEC, lambda e: e.bn_aggr(stg2[:, 12:14], stg2[:, 0:12]))
                b.op(VEC, lambda e: e.tensor_scalar(
                    stg2[:, 15:16], stg2[:, 13:14], EPS, None, AL.add))
                stok = b.op(SCA, lambda e: e.activation(
                    stg2[:, 16:17], stg2[:, 15:16], AF.Sqrt, bias=0.0, scale=1.0),
                    waits=[(ct[VEC], ct[VEC].n)])
                b.op(VEC, lambda e: e.reciprocal(stg2[:, 17:18], stg2[:, 16:17]),
                     waits=[stok])
                b.op(VEC, lambda e: e.tensor_scalar(
                    stg2[:, 18:19], stg2[:, 12:13], stg2[:, 17:18], -1.0,
                    AL.mult, AL.mult))
                for oc in range(2):
                    st2 = b.op(SCA, lambda e, oc=oc: e.activation(
                        stg[:, oc * 512:(oc + 1) * 512],
                        stg[:, oc * 512:(oc + 1) * 512], AF.Identity,
                        bias=stg2[:, 18:19], scale=stg2[:, 17:18]),
                        waits=[(ct[VEC], ct[VEC].n)])
                    vt = b.op(VEC, lambda e, oc=oc: e.tensor_mul(
                        stg[:, oc * 512:(oc + 1) * 512],
                        stg[:, oc * 512:(oc + 1) * 512],
                        bcst[:, oc * 512:(oc + 1) * 512]), waits=[st2])
                    vt = b.op(VEC, lambda e, oc=oc: e.tensor_add(
                        stg[:, oc * 512:(oc + 1) * 512],
                        stg[:, oc * 512:(oc + 1) * 512],
                        bcst[:, D + oc * 512:D + (oc + 1) * 512]))
                    b.op(VEC, lambda e, tt=tt, oc=oc: e.tensor_add(
                        xres[:, tt * D + oc * 512:tt * D + (oc + 1) * 512],
                        xres[:, tt * D + oc * 512:tt * D + (oc + 1) * 512],
                        stg[:, oc * 512:(oc + 1) * 512]))
            b.barrier()

        def emit_l1attn_all():
            emit_l1attn()
            emit_l1attn_inner()
            emit_l1attn_out()
        emit_wquant()
        b.barrier()
        stages = [lambda: emit_l0ffn(0), emit_l0attn, lambda: emit_l0ffn(1),
                  lambda: emit_l1ffn(0), emit_l1attn_all, lambda: emit_l1ffn(1)]
        for i in range(min(n_stages, len(stages))):
            stages[i]()
            b.barrier()

        for t in range(NT):
            b.dma(SYN, lambda e, t=t: e.dma_start(
                out=y_out[t * 128:(t + 1) * 128, :], in_=xres[:, t * D:(t + 1) * D]),
                inc=d_out)
        b.waits(SYN, [(d_out, d_out.n)])

        blk = ctx.enter_context(nc.Block())
        for eng in B.ENGINES:
            def mk(eng):
                def emitter(e):
                    for f in b.q[eng]:
                        f(e)
                return emitter
            getattr(blk, eng)(mk(eng))

    return nc


def _gammas():
    lin = np.linspace(np.log(1.0 / 512.0), np.log(1.0 / 32.0), NH)
    return (1.0 - np.exp(lin)).astype(np.float32)


_NC_CACHE = {}


def kernel(**inputs):
    n_stages = DEV_STAGES
    if n_stages not in _NC_CACHE:
        _NC_CACHE[n_stages] = build_nc(n_stages)
    nc = _NC_CACHE[n_stages]

    x = np.ascontiguousarray(np.asarray(inputs["L0_attn_x"], dtype=np.float32)
                             .reshape(4 * S, D))
    f32 = lambda a: np.ascontiguousarray(np.asarray(a, dtype=np.float32))

    shared = {
        "wg": f32(inputs["L0_ffn_w_gate"]),
        "wu": f32(inputs["L0_ffn_w_up"]),
        "wdn": f32(inputs["L0_ffn_w_down"]),
        "wgT": f32(np.asarray(inputs["L0_ffn_w_gate"]).T),
        "wuT": f32(np.asarray(inputs["L0_ffn_w_up"]).T),
        "wdnT": f32(np.asarray(inputs["L0_ffn_w_down"]).T),
        "clips": np.stack([f32(inputs["L0_ffn_clip_x"]).ravel(),
                           f32(inputs["L0_ffn_clip_down"]).ravel()], axis=1),
        "eye_c": np.eye(128, dtype=np.float32),
        "w1_l0": f32(inputs["L0_attn_w1"]),
        "w2_l0": f32(inputs["L0_attn_w2"]),
        "gnrows": np.stack([f32(inputs["L0_attn_gn_scale"]),
                            f32(inputs["L0_attn_gn_bias"])]),
        "U1_i": f32(inputs["L1_ffn_U1"]),
        "V1_i": f32(inputs["L1_ffn_V1"]),
        "U2_i": f32(inputs["L1_ffn_U2"]),
        "V2_i": f32(inputs["L1_ffn_V2"]),
        "b1r": f32(inputs["L1_ffn_b1"]).reshape(1, DFF1),
        "b2r": f32(inputs["L1_ffn_b2"]).reshape(1, D),
        "WqT": f32(np.asarray(inputs["L1_attn_Wq"]).T),
        "WkT": f32(np.asarray(inputs["L1_attn_Wk"]).T),
        "WvT": f32(np.asarray(inputs["L1_attn_Wv"]).T),
        "WoT": f32(np.asarray(inputs["L1_attn_Wo"]).T),
        "WeqT": f32(np.asarray(inputs["L1_attn_Weq"]).T),
        "WekT": f32(np.asarray(inputs["L1_attn_Wek"]).T),
        "battn": np.stack([f32(inputs["L1_attn_" + k]) for k in
                           ("bq", "bk", "bv", "bo", "beq", "bek")]),
        "apw": np.concatenate([f32(np.asarray(inputs["L1_attn_amp_w"]).T),
                               f32(np.asarray(inputs["L1_attn_ph_w"]).T)], axis=1),
        "apb": np.concatenate([f32(inputs["L1_attn_amp_b"]),
                               f32(inputs["L1_attn_ph_b"])]).reshape(1, 32),
        "wv_rows": f32(inputs["L1_attn_Wv"]),
        "mixwT": f32(np.asarray(inputs["L1_attn_mix_w"]).transpose(0, 2, 1)),
        "mixwrT": f32(np.roll(np.asarray(inputs["L1_attn_mix_w"]), 1, axis=1)
                      .transpose(0, 2, 1)),
        "mixbr": f32(inputs["L1_attn_mix_b"]).reshape(1, D),
        "lnrows": np.stack([f32(inputs["L1_attn_ln_g"]),
                            f32(inputs["L1_attn_ln_b"])]),
        "fq_dn": np.broadcast_to(
            np.exp(-DECAY * np.arange(512, dtype=np.float64)).astype(np.float32)[None, :],
            (128, 512)).copy(),
        "fq_up": np.broadcast_to(
            np.exp(DECAY * np.arange(512, dtype=np.float64)).astype(np.float32)[None, :],
            (128, 512)).copy(),
        "fek_up": np.broadcast_to(
            np.exp(DECAY * (np.arange(T) % 128).astype(np.float64)).astype(np.float32)[None, :],
            (128, T)).copy(),
        "fek_dn": np.broadcast_to(
            np.exp(-DECAY * (np.arange(T) % 128).astype(np.float64)).astype(np.float32)[None, :],
            (128, T)).copy(),
        "tdiag_c": np.exp(-2 * DECAY * (np.arange(128)[:, None]
                          - np.arange(128)[None, :]).clip(0, None)).astype(np.float32),
    }
    blk = np.zeros((3 * NPT, 128, 128), np.float32)
    for wi, key in enumerate(("L0_attn_wq", "L0_attn_wk", "L0_attn_wv")):
        w = f32(inputs[key])
        for j in range(NPT):
            blk[wi * NPT + j, 0:64, 0:64] = w[2 * j]
            blk[wi * NPT + j, 64:128, 64:128] = w[2 * j + 1]
    shared["wqkv_blk"] = blk
    gam = _gammas().astype(np.float64)
    ti = np.arange(128)
    gm = np.zeros((NH, 128, 128), np.float32)
    for h in range(NH):
        g = float(gam[h])
        dd = ti[None, :] - ti[:, None]
    for h in range(NH):
        g = float(gam[h])
        dd = (ti[None, :] - ti[:, None]).astype(np.float64)
        gm[h] = np.where(dd >= 0, np.exp(dd * np.log(g)), 0.0).astype(np.float32)
    shared["gmaskT"] = gm
    shared["gkvec"] = (gam[None, :] ** (127 - ti[:, None])).astype(np.float32)
    shared["gqvec"] = (gam[None, :] ** (ti[:, None] + 1)).astype(np.float32)
    shared["gcpow"] = (gam[None, :] ** 128).astype(np.float32)
    bo = np.zeros((NPT, 128, NH), np.float32)
    for j in range(NPT):
        for h in range(NH):
            lo = h * 64 - j * 128
            if 0 <= lo and lo + 64 <= 128:
                bo[j, lo:lo + 64, h] = 1.0
    shared["blkones"] = bo

    i512 = (np.arange(512) % 128).astype(np.float64)
    shared["fek_up"] = np.broadcast_to(
        np.exp(DECAY * i512).astype(np.float32)[None, :], (128, 512)).copy()
    shared["fek_dn"] = np.broadcast_to(
        np.exp(-DECAY * i512).astype(np.float32)[None, :], (128, 512)).copy()
    iq = np.arange(512, dtype=np.float64)
    shared["fq_dn"] = np.broadcast_to(
        (np.exp(-DECAY * iq) / 16.0).astype(np.float32)[None, :], (128, 512)).copy()
    shared["fq_up"] = np.broadcast_to(
        (np.exp(DECAY * iq) / 16.0).astype(np.float32)[None, :], (128, 512)).copy()
    OFFS = [0, 1, 2, 3, 4, -3, -2, -1]
    EX_K = NH * 65 * T
    EX_E = NPT * 128 * T
    EX_M = 8 * NT * 128 * 130
    EX_TOT = EX_K + 2 * EX_E + EX_M
    in_maps = []
    for c in range(8):
        m = dict(shared)
        m["x_in"] = np.ascontiguousarray(x[c * T:(c + 1) * T, :])
        half = c % 2
        corr = np.zeros((128, NT * NH), np.float32)
        if half == 1:
            for cc in range(NT):
                corr[:, cc * NH:(cc + 1) * NH] = \
                    (gam[None, :] ** (128 * cc + ti[:, None] + 1)).astype(np.float32)
        m["corrg"] = corr
        ko = np.zeros(128, np.int32)
        ko[0] = (1 - half) * EX_TOT
        m["koffs"] = ko.reshape(1, 128)
        em = np.zeros((1, 34), np.float32)
        for g in range(4):
            for bsi, OFF in enumerate(OFFS):
                if half == 0:
                    keep = (g + OFF) >= 0
                else:
                    keep = not (OFF >= 1 and g + OFF >= 4)
                em[0, g * 8 + bsi] = 1.0 if keep else 0.0
        em[0, 32] = 1.0 if half == 1 else 0.0
        em[0, 33] = 1.0 - em[0, 32]
        m["escm"] = np.broadcast_to(em, (128, 34)).copy()
        in_maps.append(m)

    res = run_bass_kernel_spmd(nc, in_maps, core_ids=list(range(8)))
    out = np.concatenate([res.results[c]["y_out"] for c in range(8)], axis=0)
    return out.reshape(4, S, D).astype(np.float32)
